# version 06 (final)
# baseline (speedup 1.0000x reference)
import math
import numpy as np
import concourse.bass as bass
import concourse.mybir as mybir
from concourse.bass_utils import run_bass_kernel_spmd

F32 = mybir.dt.float32
BF16 = mybir.dt.bfloat16
AF = mybir.ActivationFunctionType
ALU = mybir.AluOpType
AX = mybir.AxisListType

BPC, T, C = 2, 2048, 1024
NH, HD, NL = 16, 64, 8
E, FI = 8, 2048
NJ, NU, NTB, NQ, QT = 4, 8, 4, 4, 1024
MAGIC = 12582912.0
LN_TH = math.log(10000.0)
INV2PI = 1.0 / (2.0 * math.pi)
TWO_PI = 2.0 * math.pi
FFN_ON = True
NUE = NU
PH_B = True
PH_C = True
PH_G = True
PH_GATE = True
PH_XW = True
PH_EV2 = True
DBG_DUMP = False


def nsb_of(u):
    return 4 * (u % NJ) + 4


def _cumB(u, h):
    tot = 0
    for uu in range(u):
        tot += 16 * (nsb_of(uu) + 1)
    return tot + h * (nsb_of(u) + 1) if h < 16 else tot + 16 * (nsb_of(u) + 1)


def _cumB16(u, h):
    tot = 0
    for uu in range(u):
        tot += 16 * (nsb_of(uu) + 1)
    return tot + min(h, 16) * (nsb_of(u) + 1)


SC_CUM = EXP_CUM = AV_CUM = _cumB16


def NRM_CUM(u, h):
    return 16 * u + h


def MSK_CUM(u, h):
    return 4 * (16 * u + h)


def build_program():
    nc = bass.Bass("TRN2", target_bir_lowering=False,
                   detect_race_conditions=False)

    x_d = nc.dram_tensor("x", [BPC, T, C], F32, kind="ExternalInput")
    Wq_d = nc.dram_tensor("Wq", [C, C], F32, kind="ExternalInput")
    Wk_d = nc.dram_tensor("Wk", [C, C], F32, kind="ExternalInput")
    Wv_d = nc.dram_tensor("Wv", [C, C], F32, kind="ExternalInput")
    Wout_d = nc.dram_tensor("Wout", [C, C], F32, kind="ExternalInput")
    latk_d = nc.dram_tensor("lat_k", [1, NL, HD], F32, kind="ExternalInput")
    latv_d = nc.dram_tensor("lat_v", [1, NL, HD], F32, kind="ExternalInput")
    Wlk_d = nc.dram_tensor("W_lk", [HD, C], F32, kind="ExternalInput")
    Wlv_d = nc.dram_tensor("W_lv", [HD, C], F32, kind="ExternalInput")
    Wg_d = nc.dram_tensor("Wg", [C, E], F32, kind="ExternalInput")
    W11_d = nc.dram_tensor("W11", [E, C, FI], F32, kind="ExternalInput")
    W12_d = nc.dram_tensor("W12", [E, C, FI], F32, kind="ExternalInput")
    W2_d = nc.dram_tensor("W2", [E, FI, C], F32, kind="ExternalInput")
    y_d = nc.dram_tensor("y", [BPC, T, C], F32, kind="ExternalOutput")
    wstage = nc.dram_tensor("wstage", [4, C, C], BF16, kind="Internal")
    w11s = nc.dram_tensor("w11s", [E, 16, 8, 128, 128], BF16, kind="Internal")
    w12s = nc.dram_tensor("w12s", [E, 16, 8, 128, 128], BF16, kind="Internal")
    w2s = nc.dram_tensor("w2s", [E, 16, 128, C], BF16, kind="Internal")
    xn2s = nc.dram_tensor("xn2s", [8, 128, BPC * T], BF16, kind="Internal")

    def sb(name, shape, dt):
        return nc.alloc_sbuf_tensor(name, shape, dt)

    ident = sb("ident", [128, 128], F32)
    identb = sb("identb", [128, 128], BF16)
    Ctab = sb("Ctab", [128, T], BF16)
    Stab = sb("Stab", [128, T], BF16)
    w_tm = sb("w_tm", [128, 32 * E], F32)
    Wg_sb = sb("Wg_sb", [128, 64], F32)
    ones_f = sb("ones_f", [128, 64], F32)
    ones_b = sb("ones_b", [128, 2], BF16)
    pcol = sb("pcol", [128, 1], F32)
    lncols = sb("lncols", [128, 16], F32)
    frows = sb("frows", [1, 512], F32)
    r_rows = sb("r_rows", [128, 512], F32)
    latkT = sb("latkT", [64, 8], BF16)
    latvT = sb("latvT", [64, 8], BF16)
    lk_fm = sb("lk_fm", [128, 64], BF16)
    lvhat = sb("lvhat", [8, NH * 66], BF16)

    ARENA = 171 * 1024
    arena = sb("arena", [128, ARENA], mybir.dt.uint8)
    abase = nc.lookup_mloc(arena).addr
    _off = [0]

    def at(name, shape, dt, align=64):
        o = (_off[0] + align - 1) // align * align
        nbytes = int(np.prod(shape[1:])) * mybir.dt.size(dt)
        h = nc.alloc_sbuf_tensor_at(name, shape, dt, offset=abase + o)
        _off[0] = o + nbytes
        return h

    slotA = at("slotA", [128, 8 * C], BF16)
    slotB = at("slotB", [128, 8 * C], BF16)
    kbuf = at("kbuf", [128, 8 * T], BF16)
    vhat = at("vhat", [128, 16 * 1056], BF16)
    xn1_fm = at("xn1_fm", [128, 8 * 512], BF16)
    q_fm = at("q_fm", [128, 8 * 512], BF16)
    attn_fm = at("attn_fm", [128, 8 * 512], BF16)
    x_tiles = at("x_tiles", [128, 4 * C], F32)
    Bmask = at("Bmask", [128, 4 * 512], BF16)
    sq_scr = at("sq_scr", [128, C], F32)
    xn2sc = at("xn2sc", [128, C], F32)
    xn1bf = at("xn1bf", [128, C], BF16)
    xn2stage = at("xn2stage", [128, 1024], BF16)
    gl4 = at("gl4", [128, 512], F32)
    rbsb = at("rbsb", [128, 512], F32)
    aof = at("aof", [128, 512], F32)
    vbf = at("vbf", [128, 512], BF16)
    qbf = at("qbf", [128, 512], BF16)
    kbf = at("kbf", [128, 512], BF16)
    qsw = at("qsw", [128, 512], BF16)
    pt0 = at("pt0", [128, 512], BF16)
    pt1 = at("pt1", [128, 512], BF16)
    pt2 = at("pt2", [128, 512], BF16)
    pt3 = at("pt3", [128, 512], BF16)
    ptl = at("ptl", [8, 512], BF16)
    ATTN_END = _off[0]
    assert ATTN_END <= ARENA, f"attn arena {ATTN_END}"

    _off[0] = 0
    xn2_q = at("xn2_q", [128, 8 * 1024], BF16)
    fprm = at("fprm", [128, 16 * 1024], BF16)
    acc = at("acc", [128, 8 * 1024], F32)
    w2_sb0 = at("w2_sb0", [128, 16 * 1024], BF16)
    w2_sb1 = at("w2_sb1", [128, 16 * 1024], BF16)
    w11_sb0 = at("w11_sb0", [128, 8 * 128], BF16)
    w11_sb1 = at("w11_sb1", [128, 8 * 128], BF16)
    w12_sb0 = at("w12_sb0", [128, 8 * 128], BF16)
    w12_sb1 = at("w12_sb1", [128, 8 * 128], BF16)
    sil = at("sil", [128, 1024], F32)
    FFN_END = _off[0]
    assert FFN_END <= ARENA, f"ffn arena {FFN_END}"
    w2_sb = [w2_sb0, w2_sb1]
    w11_sb = [w11_sb0, w11_sb1]
    w12_sb = [w12_sb0, w12_sb1]
    KOFF = 2 * 16384
    grows = nc.alloc_sbuf_tensor_at("grows", [1, T], F32, offset=abase + KOFF)
    rope_u = nc.alloc_sbuf_tensor_at("rope_u", [128, 512], F32,
                                     offset=abase + KOFF + 8192)
    rope_n = nc.alloc_sbuf_tensor_at("rope_n", [128, 512], F32,
                                     offset=abase + KOFF + 8192 + 2048)

    P = [nc.alloc_psum_tensor(f"P{i}", [128, 512], F32) for i in range(7)]
    PT = nc.alloc_psum_tensor("PT", [128, 1024], BF16)

    semnames = [
        "s_x", "s_wqkv", "s_wa", "s_stage", "s_wf", "s_xq", "s_ac", "s_yw",
        "s_xn2w",
        "c_pre",
        "c_vpre", "c_rope", "c_xn1", "c_evA", "c_evVh", "c_msk", "c_recip",
        "c_nrm", "c_res", "c_lnv", "c_xn2", "c_evC2d", "c_gneg", "c_gw",
        "c_fp", "c_dacc",
        "c_spre", "c_lna", "c_evQKV", "c_exp", "c_opev", "c_evC2a", "c_gexp",
        "c_sil",
        "c_sv",
        "c_tpre", "c_qkv", "c_trA", "c_trV", "c_sc", "c_av", "c_rb", "c_opmm",
        "c_trC", "c_trC2", "c_gmm", "c_gg", "c_gv", "c_dmm",
    ]
    S = {n: nc.alloc_semaphore(n) for n in semnames}

    def xt(tb):
        return x_tiles[:, C * tb:C * (tb + 1)]

    def fslc(c):
        return xn1_fm[:, 512 * c:512 * (c + 1)]

    def qslc(hp):
        return q_fm[:, 512 * hp:512 * (hp + 1)]

    def aslc(c):
        return attn_fm[:, 512 * c:512 * (c + 1)]

    def kslc(hp, lo, hi):
        return kbuf[:, T * hp + lo:T * hp + hi]

    def vslc(sb_, h, w):
        base = 1056 * sb_ + 66 * h
        return vhat[:, base:base + w]

    N_WA = 5
    N_STAGE = 4 + (E * 33 if FFN_ON else 0)
    ATT_YW = 16 * NU * NTB

    with nc.Block() as blk:

        @blk.sync
        def _(eng):
            for u in range(NUE):
                sq, t0 = u // NJ, 512 * (u % NJ)
                for tb in range(NTB):
                    if u > 0 and PH_C and PH_G:
                        eng.wait_ge(S["c_xn2"], 4 * (u - 1) + tb + 1)
                        eng.wait_ge(S["s_yw"], 16 * (4 * (u - 1) + tb + 1))
                    eng.dma_start(
                        xt(tb), x_d.ap()[sq, t0 + 128 * tb:t0 + 128 * (tb + 1), :]
                    ).then_inc(S["s_x"], 16)
                eng.wait_ge(S["s_stage"], 16 * 1)
                if u > 0:
                    eng.wait_ge(S["c_qkv"], 24 * u)
                eng.dma_start(slotA[:].rearrange("p (k n) -> p k n", k=8),
                              wstage.ap()[0].rearrange("(k p) n -> p k n", k=8)
                              ).then_inc(S["s_wqkv"], 16)
                eng.wait_ge(S["s_stage"], 16 * 2)
                if u > 0 and PH_C:
                    eng.wait_ge(S["c_opmm"], 8 * u)
                eng.dma_start(slotB[:].rearrange("p (k n) -> p k n", k=8),
                              wstage.ap()[1].rearrange("(k p) n -> p k n", k=8)
                              ).then_inc(S["s_wqkv"], 16)
                eng.wait_ge(S["s_stage"], 16 * 3)
                eng.wait_ge(S["c_qkv"], 24 * u + 8)
                eng.dma_start(slotA[:].rearrange("p (k n) -> p k n", k=8),
                              wstage.ap()[2].rearrange("(k p) n -> p k n", k=8)
                              ).then_inc(S["s_wqkv"], 16)
                eng.wait_ge(S["s_stage"], 16 * 4)
                eng.wait_ge(S["c_qkv"], 24 * u + 16)
                eng.dma_start(slotB[:].rearrange("p (k n) -> p k n", k=8),
                              wstage.ap()[3].rearrange("(k p) n -> p k n", k=8)
                              ).then_inc(S["s_wqkv"], 16)
            if not FFN_ON:
                return
            eng.wait_ge(S["s_yw"], ATT_YW)
            eng.wait_ge(S["s_xn2w"], 16 * 8 * NU)
            eng.wait_ge(S["c_gexp"], 4 * NU)
            eng.wait_ge(S["c_gw"], 4 * NU)
            eng.wait_ge(S["s_stage"], 16 * N_STAGE)
            for q in range(NQ):
                if q > 0:
                    eng.wait_ge(S["c_gv"], 128 * q)
                for c in range(8):
                    eng.dma_start(
                        xn2_q[:, 1024 * c:1024 * (c + 1)],
                        xn2s.ap()[c][:, QT * q:QT * (q + 1)],
                    ).then_inc(S["s_xq"], 16)
                for e in range(E):
                    ge = 8 * q + e
                    if ge >= 2:
                        eng.wait_ge(S["c_dmm"], 8 * (ge - 1))
                    eng.dma_start(
                        w2_sb[e % 2][:].rearrange("p (k h) -> p k h", k=16),
                        w2s.ap()[e].rearrange("k p h -> p k h"),
                    ).then_inc(S["s_wf"], 16)
                    for m in range(16):
                        gvi = 16 * ge + m
                        if gvi >= 2:
                            eng.wait_ge(S["c_gv"], gvi - 1)
                        eng.dma_start(
                            w11_sb[m % 2][:].rearrange("p (k i) -> p k i", k=8),
                            w11s.ap()[e, m].rearrange("k p i -> p k i"),
                        ).then_inc(S["s_wf"], 16)
                        eng.dma_start(
                            w12_sb[m % 2][:].rearrange("p (k i) -> p k i", k=8),
                            w12s.ap()[e, m].rearrange("k p i -> p k i"),
                        ).then_inc(S["s_wf"], 16)

        @blk.gpsimd
        def _(eng):
            IOTA = dict(allow_small_or_imprecise_dtypes=True)
            eng.iota(pcol[:], [[0, 1]], channel_multiplier=1, **IOTA)
            eng.iota(sq_scr[:, 512:640], [[1, 128]], channel_multiplier=0,
                     **IOTA)
            for r in range(4):
                eng.iota(sq_scr[:, 0:512].rearrange("p (a b) -> p a b",
                                                    a=32, b=16),
                         [[16, 32], [0, 16]], base=16 - 128 * r,
                         channel_multiplier=0, **IOTA)
                eng.iota(lncols[:, 8 + r:9 + r], [[0, 1]],
                         channel_multiplier=1, **IOTA
                         ).then_inc(S["c_pre"], 1)
                eng.wait_ge(S["c_vpre"], r + 1)
            eng.iota(frows[0:1, 0:128].rearrange("p (a i r) -> p a i r",
                                                 a=2, i=32, r=2),
                     [[0, 2], [1, 32], [0, 2]], channel_multiplier=0, **IOTA)
            eng.iota(frows[0:1, 128:256].rearrange("p (a i r) -> p a i r",
                                                   a=2, i=32, r=2),
                     [[0, 2], [0, 32], [1, 2]], channel_multiplier=0, **IOTA)
            eng.iota(grows[:], [[1, T]], channel_multiplier=0, **IOTA)
            eng.memset(ones_f[:], 1.0)
            eng.memset(ones_b[:], 1.0)
            eng.memset(vhat[:], 0.0)
            eng.memset(lvhat[:], 0.0)
            eng.memset(vhat[:].rearrange("p (s h d) -> p s h d",
                                         s=16, h=16, d=66)[:, :, :, 64:65], 1.0)
            eng.memset(lvhat[:].rearrange("p (h d) -> p h d",
                                          h=16, d=66)[:, :, 64:65], 1.0
                       ).then_inc(S["c_pre"], 1)
            eng.dma_start(xn1_fm[0:64, 0:1024], Wlk_d.ap()
                          ).then_inc(S["s_wa"], 16)
            eng.dma_start(xn1_fm[0:64, 1024:2048], Wlv_d.ap()
                          ).then_inc(S["s_wa"], 16)
            eng.dma_start(sq_scr[0:8, 640:704], latk_d.ap()[0]
                          ).then_inc(S["s_wa"], 16)
            eng.dma_start(sq_scr[0:8, 704:768], latv_d.ap()[0]
                          ).then_inc(S["s_wa"], 16)
            eng.dma_start(Wg_sb[:].rearrange("p (k e) -> p k e", k=8),
                          Wg_d.ap().rearrange("(k p) e -> p k e", k=8)
                          ).then_inc(S["s_wa"], 16)
            for i, wd in enumerate([Wq_d, Wk_d, Wv_d, Wout_d]):
                eng.dma_start(wstage.ap()[i], wd.ap()).then_inc(S["s_stage"], 16)
            if FFN_ON:
                for e in range(E):
                    eng.dma_start(w2s.ap()[e].rearrange("k p h -> (k p) h"),
                                  W2_d.ap()[e]).then_inc(S["s_stage"], 16)
                    for m in range(16):
                        eng.dma_start(
                            w11s.ap()[e, m].rearrange("k p i -> (k p) i"),
                            W11_d.ap()[e][:, 128 * m:128 * (m + 1)],
                        ).then_inc(S["s_stage"], 16)
                        eng.dma_start(
                            w12s.ap()[e, m].rearrange("k p i -> (k p) i"),
                            W12_d.ap()[e][:, 128 * m:128 * (m + 1)],
                        ).then_inc(S["s_stage"], 16)
            for u in range(NUE if PH_C else 0):
                sq, t0 = u // NJ, 512 * (u % NJ)
                eng.wait_ge(S["c_res"], 8 * (u + 1))
                for tb in range(NTB):
                    eng.dma_start(
                        y_d.ap()[sq, t0 + 128 * tb:t0 + 128 * (tb + 1), :],
                        xt(tb)).then_inc(S["s_yw"], 16)
                for tb in range(NTB if (PH_G and PH_XW) else 0):
                    for cb in range(2):
                        bidx = 2 * (4 * u + tb) + cb
                        eng.wait_ge(S["c_evC2d"], bidx + 1)
                        slot = xn2stage[:, 512 * (bidx % 2):512 * (bidx % 2) + 512]
                        gtb = 4 * u + tb
                        dst = xn2s.ap()[4 * cb:4 * cb + 4][:, :,
                                                           128 * gtb:128 * (gtb + 1)
                                                           ].rearrange(
                            "c p t -> p c t")
                        eng.dma_start(
                            dst, slot.rearrange("p (c t) -> p c t", c=4)
                        ).then_inc(S["s_xn2w"], 16)
            if DBG_DUMP == 2:
                eng.wait_ge(S["c_gw"], 32)
                eng.dma_start(y_d.ap()[1, 0:128, 0:256], w_tm[:]
                              ).then_inc(S["s_yw"], 16)
                eng.wait_ge(S["s_xn2w"], 16 * 64)
                eng.dma_start(y_d.ap()[1, 128:256, :],
                              xn2s.ap()[0][:, 0:1024]).then_inc(S["s_yw"], 16)
                eng.dma_start(y_d.ap()[1, 256:384, :],
                              xn2s.ap()[5][:, 1024:2048]).then_inc(S["s_yw"], 16)
                eng.wait_ge(S["s_yw"], 16 * (32 + 3))
                return
            if DBG_DUMP:
                if PH_C:
                    eng.wait_ge(S["c_res"], 8)
                    eng.dma_start(y_d.ap()[1, 0:128, :], x_tiles[:, 0:1024]
                                  ).then_inc(S["s_yw"], 16)
                    eng.dma_start(y_d.ap()[1, 128:256, 0:512], aof[:, 0:512]
                                  ).then_inc(S["s_yw"], 16)
                eng.wait_ge(S["c_nrm"], 16)
                eng.dma_start(y_d.ap()[0, 0:128, :], attn_fm[:, 0:1024]
                              ).then_inc(S["s_yw"], 16)
                eng.dma_start(y_d.ap()[0, 128:256, :], q_fm[:, 0:1024]
                              ).then_inc(S["s_yw"], 16)
                eng.dma_start(y_d.ap()[0, 256:384, :], xn1_fm[:, 0:1024]
                              ).then_inc(S["s_yw"], 16)
                eng.dma_start(y_d.ap()[0, 384:512, 0:512], kbuf[:, 0:512]
                              ).then_inc(S["s_yw"], 16)
                eng.dma_start(y_d.ap()[0, 512:640, :], vhat[:, 0:1024]
                              ).then_inc(S["s_yw"], 16)
                eng.dma_start(y_d.ap()[0, 640:768, :], Ctab[:, 0:1024]
                              ).then_inc(S["s_yw"], 16)
                eng.wait_ge(S["s_yw"], 16 * (8 if PH_C else 6))
                return
            if FFN_ON:
                for q in range(NQ):
                    sq, t0 = q // 2, QT * (q % 2)
                    for tb in range(8):
                        eng.wait_ge(S["s_yw"], ATT_YW)
                        if q > 0:
                            eng.wait_ge(S["s_yw"],
                                        ATT_YW + 16 * (8 * (q - 1) + tb + 1))
                        eng.dma_start(
                            acc[:, 1024 * tb:1024 * (tb + 1)],
                            y_d.ap()[sq, t0 + 128 * tb:t0 + 128 * (tb + 1), :],
                        ).then_inc(S["s_ac"], 16)
                    for tb in range(8):
                        eng.wait_ge(S["c_dacc"], 64 * q + 56 + tb + 1)
                        eng.dma_start(
                            y_d.ap()[sq, t0 + 128 * tb:t0 + 128 * (tb + 1), :],
                            acc[:, 1024 * tb:1024 * (tb + 1)],
                        ).then_inc(S["s_yw"], 16)
                eng.wait_ge(S["s_yw"], ATT_YW + 16 * 8 * NQ)
            else:
                eng.wait_ge(S["s_yw"], 16 * 4 * NUE if PH_C else 0)

        @blk.tensor
        def _(eng):
            eng.wait_ge(S["s_wa"], 16 * N_WA)
            eng.wait_ge(S["c_pre"], 5)
            eng.wait_ge(S["c_vpre"], 5)
            eng.transpose(P[6][0:64, 0:8], sq_scr[0:8, 640:704],
                          ident[0:8, 0:8])
            eng.transpose(P[6][0:64, 8:16], sq_scr[0:8, 704:768],
                          ident[0:8, 0:8]).then_inc(S["c_tpre"], 1)
            eng.wait_ge(S["c_vpre"], 7)
            for hp in range(8):
                eng.matmul(P[0][:, 8 * hp:8 * hp + 8],
                           xn1_fm[0:64, 128 * hp:128 * (hp + 1)], latkT[:, :],
                           start=True, stop=True)
            eng.matmul(P[1][0:8, 0:512], latvT[:, :], xn1_fm[0:64, 1024:1536],
                       start=True, stop=True)
            eng.matmul(P[2][0:8, 0:512], latvT[:, :], xn1_fm[0:64, 1536:2048],
                       start=True, stop=True).then_inc(S["c_tpre"], 1)
            eng.wait_ge(S["c_vpre"], 6)
            for idx in range(8):
                eng.wait_ge(S["c_rope"], idx)
                s_, w_ = idx // 2, idx % 2
                eng.matmul(P[3][:, :],
                           frows[0:1, 256 * w_:256 * w_ + 128],
                           grows[:, 512 * s_:512 * (s_ + 1)],
                           start=True, stop=True).then_inc(S["c_tpre"], 1)

            for u in range(NUE):
                j = u % NJ
                nsb = nsb_of(u)
                for tb in range(NTB):
                    idx = 4 * u + tb
                    eng.wait_ge(S["c_xn1"], idx + 1)
                    for half in range(2):
                        bidx = 2 * idx + half
                        if bidx > 0:
                            eng.wait_ge(S["c_evA"], bidx)
                        if u > 0 and tb == 0 and half == 0:
                            eng.wait_ge(S["c_evVh"], 8 * u)
                        last = None
                        for cc in range(4):
                            c = 4 * half + cc
                            last = eng.transpose(
                                PT[:, 128 * cc:128 * (cc + 1)],
                                xn1bf[:, 128 * c:128 * (c + 1)], identb[:, :])
                        last.then_inc(S["c_trA"], 1)
                for gi in range(24):
                    proj, hp = gi // 8, gi % 8
                    gno = 24 * u + gi
                    bank = P[gi % 3]
                    eng.wait_ge(S["s_wqkv"], 16 * (4 * u + min(proj + 1, 3)))
                    if gi == 0:
                        eng.wait_ge(S["c_evA"], 8 * (u + 1))
                        if u == 0:
                            eng.wait_ge(S["c_vpre"], 8)
                    if gi == 2 and u > 0:
                        eng.wait_ge(S["c_gexp"], 4 * u)
                    if gno >= 3:
                        eng.wait_ge(S["c_evQKV"], gno - 2)
                    wsl = slotA if proj in (0, 2) else slotB
                    last = None
                    for kk in range(8):
                        last = eng.matmul(
                            bank[:, :],
                            wsl[:, 1024 * kk + 128 * hp:
                                1024 * kk + 128 * hp + 128],
                            fslc(kk), start=(kk == 0), stop=(kk == 7))
                    last.then_inc(S["c_qkv"], 1)

                    def _vtr(hp_):
                        i2 = 8 * u + hp_
                        eng.wait_ge(S["c_evQKV"], 24 * u + 16 + hp_ + 1)
                        if i2 > 0:
                            eng.wait_ge(S["c_evVh"], i2)
                        last = None
                        for sb_ in range(4):
                            last = eng.transpose(
                                PT[:, 128 * sb_:128 * (sb_ + 1)],
                                vbf[:, 128 * sb_:128 * (sb_ + 1)], identb[:, :])
                        last.then_inc(S["c_trV"], 1)
                    if proj == 2 and hp >= 1:
                        _vtr(hp - 1)
                    if proj == 2 and hp == 7:
                        _vtr(7)
                for h in range(16 if PH_B else 0):
                    hp, hh = h // 2, h % 2
                    row0 = 64 * hh
                    tp = (64, 0) if hh else None
                    eng.wait_ge(S["c_rope"], 8 + 16 * u + 8 + hp + 1)
                    eng.wait_ge(S["c_nrm"], NRM_CUM(u, h) if hh == 0
                                else max(NRM_CUM(u, h) - 1, 0))
                    eng.wait_ge(S["c_evVh"], 8 * u + hp + 1)
                    eng.matmul(P[6][0:8, :],
                               lk_fm[row0:row0 + 64, 8 * hp:8 * hp + 8],
                               qslc(hp)[row0:row0 + 64, :],
                               start=True, stop=True,
                               tile_position=tp).then_inc(S["c_sc"], 1)
                    def _score(sb_):
                        if sb_ >= 3:
                            eng.wait_ge(S["c_exp"], EXP_CUM(u, h) + sb_ - 1)
                        eng.matmul(P[(4, 5, 0)[sb_ % 3]][:, :],
                                   kslc(hp, 128 * sb_, 128 * (sb_ + 1))
                                   [row0:row0 + 64, :],
                                   qslc(hp)[row0:row0 + 64, :],
                                   start=True, stop=True,
                                   tile_position=tp).then_inc(S["c_sc"], 1)
                    if hh == 1:
                        eng.wait_ge(S["c_exp"], EXP_CUM(u, h) + 1)

                    def _av(sb_):
                        pt = (pt0, pt1, pt2, pt3)[sb_ % 4]
                        if hh == 1:
                            eng.matmul(P[6][32:33, :], ones_b[:, 0:1],
                                       pt[:, :], start=(sb_ == 0), stop=False,
                                       tile_position=(0, 32))
                            eng.matmul(P[6][64:128, :], vslc(sb_, h, 64),
                                       pt[:, :], start=(sb_ == 0), stop=False,
                                       tile_position=(0, 64)
                                       ).then_inc(S["c_av"], 1)
                        else:
                            eng.matmul(P[2][0:66, :], vslc(sb_, h, 66),
                                       pt[:, :], start=(sb_ == 0), stop=False
                                       ).then_inc(S["c_av"], 1)

                    def _ptwait(sb_):
                        if sb_ < 4 * j:
                            eng.wait_ge(S["c_exp"], EXP_CUM(u, h) + sb_ + 2)
                        else:
                            eng.wait_ge(S["c_msk"],
                                        MSK_CUM(u, h) + (sb_ - 4 * j) + 1)

                    eng.wait_ge(S["c_exp"], EXP_CUM(u, h))
                    for q_ in range(min(3, nsb)):
                        _score(q_)
                    sb_ = 0
                    while sb_ < nsb:
                        hi = min(sb_ + 3, nsb)
                        for cur in range(sb_, hi):
                            _ptwait(cur)
                        for cur in range(sb_ + 3, min(sb_ + 6, nsb)):
                            _score(cur)
                        for cur in range(sb_, hi):
                            _av(cur)
                        sb_ += 3
                    eng.wait_ge(S["c_exp"], EXP_CUM(u, h) + 1)
                    if hh == 1:
                        eng.matmul(P[6][32:33, :], ones_b[0:8, 0:1], ptl[:, :],
                                   start=False, stop=True,
                                   tile_position=(0, 32))
                        eng.matmul(P[6][64:128, :],
                                   lvhat[:, 66 * h:66 * h + 64], ptl[:, :],
                                   start=False, stop=True,
                                   tile_position=(0, 64)).then_inc(S["c_av"], 1)
                    else:
                        eng.matmul(P[2][0:66, :], lvhat[:, 66 * h:66 * h + 66],
                                   ptl[:, :], start=False, stop=True
                                   ).then_inc(S["c_av"], 1)
                    eng.wait_ge(S["c_recip"], NRM_CUM(u, h) + 1)
                    if hh == 0:
                        eng.matmul(P[3][0:64, :], ones_f[64:65, 0:64],
                                   r_rows[64:65, :], start=True, stop=True,
                                   tile_position=(64, 0)).then_inc(S["c_rb"], 1)
                    else:
                        eng.matmul(P[1][64:128, :], ones_f[32:33, 0:64],
                                   r_rows[32:33, :], start=True, stop=True,
                                   tile_position=(32, 64)).then_inc(S["c_rb"], 1)
                for c in range(8 if PH_C else 0):
                    oidx = 8 * u + c
                    eng.wait_ge(S["s_wqkv"], 16 * (4 * u + 4))
                    if c == 0:
                        eng.wait_ge(S["c_exp"], EXP_CUM(u, 16))
                        eng.wait_ge(S["c_nrm"], 16 * (u + 1))
                        eng.wait_ge(S["c_recip"], 16 * (u + 1))
                    if oidx >= 2:
                        eng.wait_ge(S["c_opev"], oidx - 1)
                    if c == 0:
                        eng.wait_ge(S["c_evC2d"], 8 * u)
                        eng.wait_ge(S["c_evC2a"], 8 * u)
                    last = None
                    for kk in range(8):
                        last = eng.matmul(
                            P[oidx % 2][:, :],
                            slotB[:, 1024 * kk + 128 * c:
                                  1024 * kk + 128 * c + 128],
                            aslc(kk), start=(kk == 0), stop=(kk == 7))
                    last.then_inc(S["c_opmm"], 1)
                    eng.wait_ge(S["c_opev"], oidx + 1)
                    if oidx > 0:
                        eng.wait_ge(S["c_res"], oidx)
                    last = None
                    for tb in range(NTB):
                        last = eng.transpose(
                            P[6][:, 128 * tb:128 * (tb + 1)],
                            aof[:, 128 * tb:128 * (tb + 1)], ident[:, :])
                    last.then_inc(S["c_trC"], 1)
                for tb in range(NTB if (PH_C and PH_G) else 0):
                    idx = 4 * u + tb
                    for half in range(2):
                        bidx = 2 * idx + half
                        eng.wait_ge(S["c_xn2"], idx + 1)
                        if tb == 0 and half == 0:
                            eng.wait_ge(S["c_res"], 8 * (u + 1))
                        if bidx > 0:
                            eng.wait_ge(S["c_evC2d"], bidx)
                            eng.wait_ge(S["c_evC2a"], bidx)
                        last = None
                        for cc in range(4):
                            c = 4 * half + cc
                            last = eng.transpose(
                                P[6][:, 128 * cc:128 * (cc + 1)],
                                xn2sc[:, 128 * c:128 * (c + 1)], ident[:, :])
                        last.then_inc(S["c_trC2"], 1)
                        if not PH_GATE:
                            continue
                        eng.wait_ge(S["c_evC2a"], bidx + 1)
                        if tb == 0 and half == 0:
                            eng.wait_ge(S["c_nrm"], 16 * (u + 1))
                            eng.wait_ge(S["c_recip"], 16 * (u + 1))
                        if half == 0 and idx > 0:
                            eng.wait_ge(S["c_gexp"], idx)
                        last = None
                        for cc in range(4):
                            c = 4 * half + cc
                            last = eng.matmul(
                                P[2][:, 0:8], gl4[:, 128 * cc:128 * (cc + 1)],
                                Wg_sb[:, 8 * c:8 * (c + 1)],
                                start=(c == 0), stop=(c == 7))
                        last.then_inc(S["c_gmm"], 1)

            if not FFN_ON:
                return
            eng.wait_ge(S["c_gexp"], 4 * NU)
            eng.wait_ge(S["c_nrm"], 16 * NU)
            eng.wait_ge(S["c_opev"], 8 * NU)
            eng.wait_ge(S["c_evC2a"], 8 * NU)
            eng.wait_ge(S["c_evC2d"], 8 * NU)
            eng.wait_ge(S["c_res"], 8 * NU)
            for q in range(NQ):
                for e in range(E):
                    ge = 8 * q + e
                    for m in range(16):
                        gvi = 16 * ge + m
                        pgs = (P[0], P[1]) if m % 2 == 0 else (P[2], P[3])
                        wb11, wb12 = w11_sb[m % 2], w12_sb[m % 2]
                        eng.wait_ge(S["s_wf"], 16 * (33 * ge + 1 + 2 * (m + 1)))
                        if m == 0:
                            eng.wait_ge(S["s_xq"], 16 * 8 * (q + 1))
                        if gvi >= 2:
                            eng.wait_ge(S["c_sil"], gvi - 1)
                        if m == 0 and ge > 0:
                            eng.wait_ge(S["c_dacc"], 8 * ge)
                        last = None
                        for kk in range(8):
                            st_, sp_ = (kk == 0), (kk == 7)
                            eng.matmul(pgs[0][:, :],
                                       wb11[:, 128 * kk:128 * (kk + 1)],
                                       xn2_q[:, 1024 * kk:1024 * kk + 512],
                                       start=st_, stop=sp_)
                            last = eng.matmul(
                                pgs[1][:, :], wb11[:, 128 * kk:128 * (kk + 1)],
                                xn2_q[:, 1024 * kk + 512:1024 * (kk + 1)],
                                start=st_, stop=sp_)
                        last.then_inc(S["c_gg"], 1)
                        if gvi >= 1:
                            eng.wait_ge(S["c_fp"], gvi)
                        last = None
                        for kk in range(8):
                            st_, sp_ = (kk == 0), (kk == 7)
                            eng.matmul(P[4][:, :],
                                       wb12[:, 128 * kk:128 * (kk + 1)],
                                       xn2_q[:, 1024 * kk:1024 * kk + 512],
                                       start=st_, stop=sp_)
                            last = eng.matmul(
                                P[5][:, :], wb12[:, 128 * kk:128 * (kk + 1)],
                                xn2_q[:, 1024 * kk + 512:1024 * (kk + 1)],
                                start=st_, stop=sp_)
                        last.then_inc(S["c_gv"], 1)
                    eng.wait_ge(S["s_wf"], 16 * (33 * ge + 1))
                    eng.wait_ge(S["c_fp"], 16 * (ge + 1))
                    wb2 = w2_sb[e % 2]
                    for tb in range(8):
                        di = 8 * ge + tb
                        pds = (P[0], P[1]) if tb % 2 == 0 else (P[2], P[3])
                        if tb >= 2:
                            eng.wait_ge(S["c_dacc"], di - 1)
                        last = None
                        for kk in range(16):
                            st_, sp_ = (kk == 0), (kk == 15)
                            eng.matmul(pds[0][:, :],
                                       fprm[:, 1024 * kk + 128 * tb:
                                            1024 * kk + 128 * tb + 128],
                                       wb2[:, 1024 * kk:1024 * kk + 512],
                                       start=st_, stop=sp_)
                            last = eng.matmul(
                                pds[1][:, :],
                                fprm[:, 1024 * kk + 128 * tb:
                                     1024 * kk + 128 * tb + 128],
                                wb2[:, 1024 * kk + 512:1024 * (kk + 1)],
                                start=st_, stop=sp_)
                        last.then_inc(S["c_dmm"], 1)

        @blk.vector
        def _(eng):
            sv = [0]
            for r in range(4):
                eng.wait_ge(S["c_pre"], r + 1)
                eng.tensor_scalar(Bmask[:, 512 * r:512 * (r + 1)],
                                  sq_scr[:, 0:512], lncols[:, 8 + r:9 + r],
                                  None, ALU.is_gt).then_inc(S["c_vpre"], 1)
            eng.tensor_scalar(ident[:], sq_scr[:, 512:640], pcol[:], None,
                              ALU.is_equal).then_inc(S["c_sv"], 1)
            sv[0] += 1
            eng.wait_ge(S["c_sv"], sv[0])
            eng.tensor_copy(identb[:], ident[:]).then_inc(S["c_vpre"], 1)
            eng.wait_ge(S["c_spre"], 1)
            eng.tensor_scalar(frows[0:1, 128:256], frows[0:1, 128:256], 2.0,
                              -1.0, ALU.mult, ALU.add).then_inc(S["c_sv"], 1)
            sv[0] += 1
            eng.wait_ge(S["c_sv"], sv[0])
            eng.tensor_mul(frows[0:1, 256:384], frows[0:1, 0:128],
                           frows[0:1, 128:256]).then_inc(S["c_sv"], 1)
            sv[0] += 1
            eng.wait_ge(S["c_sv"], sv[0])
            eng.tensor_scalar(frows[0:1, 0:128], frows[0:1, 0:128], INV2PI,
                              None, ALU.mult)
            eng.tensor_scalar(frows[0:1, 256:384], frows[0:1, 256:384],
                              INV2PI, None, ALU.mult).then_inc(S["c_vpre"], 1)
            eng.wait_ge(S["c_tpre"], 1)
            eng.tensor_copy(latkT[:], P[6][0:64, 0:8])
            eng.tensor_copy(latvT[:], P[6][0:64, 8:16]).then_inc(S["c_vpre"], 1)
            eng.wait_ge(S["c_tpre"], 2)
            eng.tensor_copy(lk_fm[:], P[0][:, 0:64])
            eng.tensor_copy(
                lvhat[:].rearrange("p (h d) -> p h d", h=16, d=66)
                [:, 0:8, 0:64],
                P[1][0:8, 0:512].rearrange("p (h d) -> p h d", h=8, d=64))
            eng.tensor_copy(
                lvhat[:].rearrange("p (h d) -> p h d", h=16, d=66)
                [:, 8:16, 0:64],
                P[2][0:8, 0:512].rearrange("p (h d) -> p h d", h=8, d=64)
            ).then_inc(S["c_vpre"], 1)
            for idx in range(8):
                eng.wait_ge(S["c_tpre"], 2 + idx + 1)
                if idx > 0:
                    eng.wait_ge(S["c_spre"], 1 + idx)
                w_ = idx % 2
                if w_ == 0:
                    last = eng.tensor_scalar(rope_u[:], P[3][:, :], 0.25, None,
                                             ALU.add)
                else:
                    last = eng.tensor_copy(rope_u[:], P[3][:, :])
                last.then_inc(S["c_sv"], 1)
                sv[0] += 1
                eng.wait_ge(S["c_sv"], sv[0])
                eng.tensor_scalar(rope_n[:], rope_u[:], MAGIC, MAGIC,
                                  ALU.add, ALU.subtract).then_inc(S["c_sv"], 1)
                sv[0] += 1
                eng.wait_ge(S["c_sv"], sv[0])
                eng.scalar_tensor_tensor(rope_u[:], rope_n[:], -1.0,
                                         rope_u[:], ALU.mult, ALU.add
                                         ).then_inc(S["c_rope"], 1)

            mask32 = [i ^ 1 for i in range(32)]
            for u in range(NUE):
                j = u % NJ
                nsb = nsb_of(u)
                for tb in range(NTB):
                    idx = 4 * u + tb
                    eng.wait_ge(S["s_x"], 16 * (idx + 1))
                    if idx == 0:
                        eng.wait_ge(S["c_tpre"], 1)
                    x_ = xt(tb)
                    def _sg(ins_):
                        ins_.then_inc(S["c_sv"], 1)
                        sv[0] += 1
                        eng.wait_ge(S["c_sv"], sv[0])
                    _sg(eng.reduce_sum(lncols[:, 0:1], x_, axis=AX.X))
                    _sg(eng.tensor_mul(sq_scr[:], x_, x_))
                    _sg(eng.reduce_sum(lncols[:, 1:2], sq_scr[:], axis=AX.X))
                    _sg(eng.tensor_scalar(lncols[:, 2:3], lncols[:, 0:1],
                                          1.0 / C, None, ALU.mult))
                    _sg(eng.tensor_scalar(lncols[:, 3:4], lncols[:, 1:2],
                                          1.0 / C, None, ALU.mult))
                    _sg(eng.tensor_scalar(lncols[:, 4:5], lncols[:, 2:3],
                                          lncols[:, 2:3], None, ALU.mult))
                    eng.tensor_scalar(lncols[:, 5:6], lncols[:, 3:4],
                                      lncols[:, 4:5], 1e-5, ALU.subtract,
                                      ALU.add).then_inc(S["c_lnv"], 1)
                    eng.wait_ge(S["c_lna"], 8 * u + tb + 1)
                    eng.reciprocal(lncols[:, 7:8], lncols[:, 6:7]
                                   ).then_inc(S["c_sv"], 1)
                    sv[0] += 1
                    eng.wait_ge(S["c_sv"], sv[0])
                    if idx > 0:
                        eng.wait_ge(S["c_trA"], 2 * idx)
                    eng.tensor_scalar(xn1bf[:], x_, lncols[:, 2:3],
                                      lncols[:, 7:8], ALU.subtract, ALU.mult
                                      ).then_inc(S["c_xn1"], 1)
                    for half in range(2):
                        bidx = 2 * idx + half
                        eng.wait_ge(S["c_trA"], bidx + 1)
                        if bidx == 0:
                            eng.wait_ge(S["c_tpre"], 2)
                        if u > 0 and tb == 0 and half == 0:
                            eng.wait_ge(S["c_qkv"], 24 * u)
                        last = None
                        for cc in range(4):
                            c = 4 * half + cc
                            last = eng.tensor_copy(
                                fslc(c)[:, 128 * tb:128 * (tb + 1)],
                                PT[:, 128 * cc:128 * (cc + 1)])
                        last.then_inc(S["c_evA"], 1)
                for hp in range(8):
                    eng.wait_ge(S["c_evQKV"], 24 * u + hp + 1)
                    if u == 0 and hp == 0:
                        eng.wait_ge(S["c_spre"], 9)
                    if u > 0:
                        eng.wait_ge(S["c_sc"], SC_CUM(u - 1, 2 * hp + 2))
                    eng.stream_shuffle(qsw[:], qbf[:], mask32
                                       ).then_inc(S["c_sv"], 1)
                    sv[0] += 1
                    eng.wait_ge(S["c_sv"], sv[0])
                    eng.tensor_mul(qsw[:], qsw[:], Stab[:, 512 * j:512 * (j + 1)])
                    eng.tensor_mul(qbf[:], qbf[:], Ctab[:, 512 * j:512 * (j + 1)]
                                   ).then_inc(S["c_sv"], 1)
                    sv[0] += 1
                    eng.wait_ge(S["c_sv"], sv[0])
                    eng.tensor_add(qslc(hp), qbf[:], qsw[:]
                                   ).then_inc(S["c_rope"], 1)
                for hp in range(8):
                    eng.wait_ge(S["c_evQKV"], 24 * u + 8 + hp + 1)
                    if u >= 4 and hp == 0:
                        eng.wait_ge(S["c_sc"], SC_CUM(4, 0))
                    eng.stream_shuffle(qsw[:], kbf[:], mask32
                                       ).then_inc(S["c_sv"], 1)
                    sv[0] += 1
                    eng.wait_ge(S["c_sv"], sv[0])
                    eng.tensor_mul(qsw[:], qsw[:], Stab[:, 512 * j:512 * (j + 1)])
                    eng.tensor_mul(kbf[:], kbf[:], Ctab[:, 512 * j:512 * (j + 1)]
                                   ).then_inc(S["c_sv"], 1)
                    sv[0] += 1
                    eng.wait_ge(S["c_sv"], sv[0])
                    eng.tensor_add(kslc(hp, 512 * j, 512 * (j + 1)),
                                   kbf[:], qsw[:]).then_inc(S["c_rope"], 1)
                for hp in range(8):
                    i2 = 8 * u + hp
                    eng.wait_ge(S["c_trV"], i2 + 1)
                    if u >= 4 and hp == 0:
                        eng.wait_ge(S["c_av"], AV_CUM(4, 0))
                    last = None
                    for sb_ in range(4):
                        vsrc = PT[:, 128 * sb_:128 * (sb_ + 1)].rearrange(
                            "p (h d) -> p h d", h=2, d=64)
                        base = 1056 * (4 * j + sb_) + 132 * hp
                        vdst = vhat[:, base:base + 132].rearrange(
                            "p (h d) -> p h d", h=2, d=66)[:, :, 0:64]
                        last = eng.tensor_copy(vdst, vsrc)
                    last.then_inc(S["c_evVh"], 1)
                for h in range(16 if PH_B else 0):
                    hp, hh = h // 2, h % 2
                    for r in range(4):
                        sb_ = 4 * j + r
                        eng.wait_ge(S["c_exp"], EXP_CUM(u, h) + sb_ + 2)
                        pt = (pt0, pt1, pt2, pt3)[sb_ % 4]
                        eng.tensor_mul(pt[:, :], pt[:, :],
                                       Bmask[:, 512 * r:512 * (r + 1)]
                                       ).then_inc(S["c_msk"], 1)
                    eng.wait_ge(S["c_av"], AV_CUM(u, h) + nsb + 1)
                    if hh == 0:
                        eng.reciprocal(r_rows[64:65, :], P[2][64:65, :]
                                       ).then_inc(S["c_recip"], 1)
                    else:
                        eng.reciprocal(r_rows[32:33, :], P[6][32:33, :]
                                       ).then_inc(S["c_recip"], 1)
                    eng.wait_ge(S["c_rb"], NRM_CUM(u, h) + 1)
                    if PH_C:
                        eng.wait_ge(S["c_opmm"], 8 * u)
                    if hh == 0:
                        eng.tensor_copy(rbsb[0:64, :], P[3][0:64, :]
                                        ).then_inc(S["c_sv"], 1)
                        sv[0] += 1
                        eng.wait_ge(S["c_sv"], sv[0])
                        eng.tensor_mul(aslc(hp)[0:64, :], P[2][0:64, :],
                                       rbsb[0:64, :]).then_inc(S["c_nrm"], 1)
                    else:
                        eng.tensor_copy(rbsb[64:128, :], P[1][64:128, :]
                                        ).then_inc(S["c_sv"], 1)
                        sv[0] += 1
                        eng.wait_ge(S["c_sv"], sv[0])
                        eng.tensor_mul(aslc(hp)[64:128, :], P[6][64:128, :],
                                       rbsb[64:128, :]).then_inc(S["c_nrm"], 1)
                for c in range(8 if PH_C else 0):
                    oidx = 8 * u + c
                    eng.wait_ge(S["c_trC"], oidx + 1)
                    last = None
                    for tb in range(NTB):
                        last = eng.scalar_tensor_tensor(
                            xt(tb)[:, 128 * c:128 * (c + 1)],
                            P[6][:, 128 * tb:128 * (tb + 1)], 1.0,
                            xt(tb)[:, 128 * c:128 * (c + 1)],
                            ALU.mult, ALU.add)
                    last.then_inc(S["c_res"], 1)
                for tb in range(NTB if (PH_C and PH_G) else 0):
                    idx = 4 * u + tb
                    x_ = xt(tb)
                    def _sg(ins_):
                        ins_.then_inc(S["c_sv"], 1)
                        sv[0] += 1
                        eng.wait_ge(S["c_sv"], sv[0])
                    _sg(eng.reduce_sum(lncols[:, 0:1], x_, axis=AX.X))
                    _sg(eng.tensor_mul(sq_scr[:], x_, x_))
                    _sg(eng.reduce_sum(lncols[:, 1:2], sq_scr[:], axis=AX.X))
                    _sg(eng.tensor_scalar(lncols[:, 2:3], lncols[:, 0:1],
                                          1.0 / C, None, ALU.mult))
                    _sg(eng.tensor_scalar(lncols[:, 3:4], lncols[:, 1:2],
                                          1.0 / C, None, ALU.mult))
                    _sg(eng.tensor_scalar(lncols[:, 4:5], lncols[:, 2:3],
                                          lncols[:, 2:3], None, ALU.mult))
                    eng.tensor_scalar(lncols[:, 5:6], lncols[:, 3:4],
                                      lncols[:, 4:5], 1e-5, ALU.subtract,
                                      ALU.add).then_inc(S["c_lnv"], 1)
                    eng.wait_ge(S["c_lna"], 8 * u + 4 + tb + 1)
                    eng.reciprocal(lncols[:, 7:8], lncols[:, 6:7]
                                   ).then_inc(S["c_sv"], 1)
                    sv[0] += 1
                    eng.wait_ge(S["c_sv"], sv[0])
                    if idx > 0:
                        eng.wait_ge(S["c_trC2"], 2 * idx)
                    eng.tensor_scalar(xn2sc[:], x_, lncols[:, 2:3],
                                      lncols[:, 7:8], ALU.subtract, ALU.mult
                                      ).then_inc(S["c_xn2"], 1)
                    for half in range(2):
                        bidx = 2 * idx + half
                        eng.wait_ge(S["c_trC2"], bidx + 1)
                        if PH_XW and bidx >= 2:
                            eng.wait_ge(S["s_xn2w"], 16 * (bidx - 1))
                        slot = xn2stage[:, 512 * (bidx % 2):
                                        512 * (bidx % 2) + 512]
                        last = None
                        if PH_EV2:
                            for cc in range(4):
                                last = eng.tensor_copy(
                                    slot[:, 128 * cc:128 * (cc + 1)],
                                    P[6][:, 128 * cc:128 * (cc + 1)])
                            last.then_inc(S["c_evC2d"], 1)
                        else:
                            eng.engine_nop().then_inc(S["c_evC2d"], 1)
                    if not PH_GATE:
                        continue
                    eng.wait_ge(S["c_gmm"], 2 * (idx + 1))
                    eng.reduce_max(lncols[:, 9:10], P[2][:, 0:8], axis=AX.X
                                   ).then_inc(S["c_sv"], 1)
                    sv[0] += 1
                    eng.wait_ge(S["c_sv"], sv[0])
                    eng.tensor_scalar(lncols[:, 10:11], lncols[:, 9:10], -1.0,
                                      None, ALU.mult).then_inc(S["c_gneg"], 1)
                    eng.wait_ge(S["c_gexp"], idx + 1)
                    eng.reciprocal(lncols[:, 12:13], lncols[:, 11:12]
                                   ).then_inc(S["c_sv"], 1)
                    sv[0] += 1
                    eng.wait_ge(S["c_sv"], sv[0])
                    gsc = sq_scr[:, 0:8]
                    scr2 = sq_scr[:, 8:16]
                    eng.tensor_scalar(gsc, gl4[:, 0:8], lncols[:, 12:13], None,
                                      ALU.mult).then_inc(S["c_sv"], 1)
                    sv[0] += 1
                    eng.wait_ge(S["c_sv"], sv[0])
                    eng.reduce_max(lncols[:, 9:10], gsc, axis=AX.X
                                   ).then_inc(S["c_sv"], 1)
                    sv[0] += 1
                    eng.wait_ge(S["c_sv"], sv[0])
                    eng.tensor_scalar(scr2, gsc, lncols[:, 9:10], None,
                                      ALU.is_equal).then_inc(S["c_sv"], 1)
                    sv[0] += 1
                    eng.wait_ge(S["c_sv"], sv[0])
                    eng.scalar_tensor_tensor(scr2, scr2, -2.0, gsc,
                                             ALU.mult, ALU.add
                                             ).then_inc(S["c_sv"], 1)
                    sv[0] += 1
                    eng.wait_ge(S["c_sv"], sv[0])
                    eng.reduce_max(lncols[:, 10:11], scr2, axis=AX.X
                                   ).then_inc(S["c_sv"], 1)
                    sv[0] += 1
                    eng.wait_ge(S["c_sv"], sv[0])
                    eng.tensor_scalar(scr2, gsc, lncols[:, 10:11], None,
                                      ALU.is_ge).then_inc(S["c_sv"], 1)
                    sv[0] += 1
                    eng.wait_ge(S["c_sv"], sv[0])
                    eng.tensor_mul(gsc, gsc, scr2).then_inc(S["c_sv"], 1)
                    sv[0] += 1
                    eng.wait_ge(S["c_sv"], sv[0])
                    eng.tensor_scalar(lncols[:, 13:14], lncols[:, 9:10],
                                      lncols[:, 10:11], 1e-8, ALU.add, ALU.add
                                      ).then_inc(S["c_sv"], 1)
                    sv[0] += 1
                    eng.wait_ge(S["c_sv"], sv[0])
                    eng.reciprocal(lncols[:, 14:15], lncols[:, 13:14]
                                   ).then_inc(S["c_sv"], 1)
                    sv[0] += 1
                    eng.wait_ge(S["c_sv"], sv[0])
                    gtb = 4 * u + tb
                    eng.tensor_scalar(w_tm[:, 8 * gtb:8 * (gtb + 1)], gsc,
                                      lncols[:, 14:15], None, ALU.mult
                                      ).then_inc(S["c_gw"], 1)

            if not FFN_ON:
                return
            eng.wait_ge(S["c_gw"], 4 * NU)
            for q in range(NQ):
                for e in range(E):
                    ge = 8 * q + e
                    for m in range(16):
                        gvi = 16 * ge + m
                        eng.wait_ge(S["c_sil"], gvi + 1)
                        eng.wait_ge(S["c_gv"], gvi + 1)
                        eng.tensor_mul(fprm[:, 1024 * m:1024 * m + 512],
                                       sil[:, 0:512], P[4][:, :])
                        eng.tensor_mul(fprm[:, 1024 * m + 512:1024 * (m + 1)],
                                       sil[:, 512:1024], P[5][:, :]
                                       ).then_inc(S["c_fp"], 1)
                    for tb in range(8):
                        di = 8 * ge + tb
                        pds = (P[0], P[1]) if tb % 2 == 0 else (P[2], P[3])
                        eng.wait_ge(S["c_dmm"], di + 1)
                        eng.wait_ge(S["s_ac"], 16 * (8 * q + tb + 1))
                        wcol = w_tm[:, 8 * (8 * q + tb) + e:
                                    8 * (8 * q + tb) + e + 1]
                        a0 = acc[:, 1024 * tb:1024 * tb + 512]
                        a1 = acc[:, 1024 * tb + 512:1024 * (tb + 1)]
                        eng.scalar_tensor_tensor(a0, pds[0][:, :], wcol, a0,
                                                 ALU.mult, ALU.add)
                        eng.scalar_tensor_tensor(a1, pds[1][:, :], wcol, a1,
                                                 ALU.mult, ALU.add
                                                 ).then_inc(S["c_dacc"], 1)

        @blk.scalar
        def _(eng):
            eng.wait_ge(S["c_pre"], 5)
            eng.activation(frows[0:1, 0:128], frows[0:1, 0:128], AF.Exp,
                           scale=-LN_TH / 32.0).then_inc(S["c_spre"], 1)
            for idx in range(8):
                eng.wait_ge(S["c_rope"], idx + 1)
                s_, w_ = idx // 2, idx % 2
                tab = Ctab if w_ == 0 else Stab
                eng.activation(tab[:, 512 * s_:512 * (s_ + 1)], rope_u[:],
                               AF.Sin, scale=TWO_PI).then_inc(S["c_spre"], 1)

            for u in range(NUE):
                j = u % NJ
                nsb = nsb_of(u)
                for tb in range(NTB):
                    eng.wait_ge(S["c_lnv"], 8 * u + tb + 1)
                    eng.activation(lncols[:, 6:7], lncols[:, 5:6], AF.Sqrt
                                   ).then_inc(S["c_lna"], 1)
                for gi in range(24):
                    proj, hp = gi // 8, gi % 8
                    gno = 24 * u + gi
                    eng.wait_ge(S["c_qkv"], gno + 1)
                    if proj == 0:
                        eng.wait_ge(S["c_rope"], 8 + 16 * u + hp)
                        eng.copy(qbf[:], P[gi % 3][:, :]).then_inc(
                            S["c_evQKV"], 1)
                    elif proj == 1:
                        eng.wait_ge(S["c_rope"], 8 + 16 * u + 8 + hp)
                        eng.copy(kbf[:], P[gi % 3][:, :]).then_inc(
                            S["c_evQKV"], 1)
                    else:
                        eng.wait_ge(S["c_trV"], 8 * u + hp)
                        eng.copy(vbf[:], P[gi % 3][:, :]).then_inc(
                            S["c_evQKV"], 1)
                for h in range(16 if PH_B else 0):
                    eng.wait_ge(S["c_sc"], SC_CUM(u, h) + 1)
                    eng.wait_ge(S["c_av"], AV_CUM(u, h))
                    eng.wait_ge(S["c_nrm"], NRM_CUM(u, h))
                    eng.activation(ptl[:, :], P[6][0:8, :], AF.Exp,
                                   scale=0.125).then_inc(S["c_exp"], 1)
                    for sb_ in range(nsb):
                        eng.wait_ge(S["c_sc"], SC_CUM(u, h) + sb_ + 2)
                        if sb_ >= 4:
                            eng.wait_ge(S["c_av"], AV_CUM(u, h) + sb_ - 3)
                        pt = (pt0, pt1, pt2, pt3)[sb_ % 4]
                        eng.activation(pt[:, :], P[(4, 5, 0)[sb_ % 3]][:, :],
                                       AF.Exp, scale=0.125
                                       ).then_inc(S["c_exp"], 1)
                for c in range(8 if PH_C else 0):
                    oidx = 8 * u + c
                    eng.wait_ge(S["c_opmm"], oidx + 1)
                    if oidx > 0:
                        eng.wait_ge(S["c_trC"], oidx)
                    eng.copy(aof[:], P[oidx % 2][:, :]).then_inc(S["c_opev"], 1)
                for tb in range(NTB if (PH_C and PH_G) else 0):
                    idx = 4 * u + tb
                    eng.wait_ge(S["c_lnv"], 8 * u + 4 + tb + 1)
                    eng.activation(lncols[:, 6:7], lncols[:, 5:6], AF.Sqrt
                                   ).then_inc(S["c_lna"], 1)
                    for half in range(2):
                        bidx = 2 * idx + half
                        eng.wait_ge(S["c_trC2"], bidx + 1)
                        eng.wait_ge(S["c_evC2d"], bidx + 1)
                        if PH_GATE and idx > 0 and half == 0:
                            eng.wait_ge(S["c_gw"], idx)
                        if PH_GATE and bidx > 0:
                            eng.wait_ge(S["c_gmm"], bidx)
                        last = None
                        if PH_EV2:
                            for cc in range(4):
                                last = eng.copy(
                                    gl4[:, 128 * cc:128 * (cc + 1)],
                                    P[6][:, 128 * cc:128 * (cc + 1)])
                            last.then_inc(S["c_evC2a"], 1)
                        else:
                            eng.nop().then_inc(S["c_evC2a"], 1)
                    if PH_GATE:
                        eng.wait_ge(S["c_gneg"], idx + 1)
                        eng.wait_ge(S["c_gmm"], 2 * (idx + 1))
                        eng.wait_ge(S["c_evC2a"], 2 * (idx + 1))
                        eng.activation(gl4[:, 0:8], P[2][:, 0:8], AF.Exp,
                                       bias=lncols[:, 10:11],
                                       accum_out=lncols[:, 11:12]
                                       ).then_inc(S["c_gexp"], 1)

            if not FFN_ON:
                return
            for q in range(NQ):
                for e in range(E):
                    ge = 8 * q + e
                    for m in range(16):
                        gvi = 16 * ge + m
                        pgs = (P[0], P[1]) if m % 2 == 0 else (P[2], P[3])
                        eng.wait_ge(S["c_gg"], gvi + 1)
                        if gvi > 0:
                            eng.wait_ge(S["c_fp"], gvi)
                        eng.activation(sil[:, 0:512], pgs[0][:, :], AF.Silu)
                        eng.activation(sil[:, 512:1024], pgs[1][:, :], AF.Silu
                                       ).then_inc(S["c_sil"], 1)

    return nc


_NC_CACHE = {}


def kernel(L0_attn_x, L0_attn_Wq, L0_attn_Wk, L0_attn_Wv, L0_attn_lat_k,
           L0_attn_lat_v, L0_attn_W_lk, L0_attn_W_lv, L0_attn_W_out,
           L0_ffn_Wg, L0_ffn_W11, L0_ffn_W12, L0_ffn_W2):
    x = np.ascontiguousarray(np.asarray(L0_attn_x, dtype=np.float32))
    weights = {
        "Wq": np.ascontiguousarray(L0_attn_Wq, dtype=np.float32),
        "Wk": np.ascontiguousarray(L0_attn_Wk, dtype=np.float32),
        "Wv": np.ascontiguousarray(L0_attn_Wv, dtype=np.float32),
        "Wout": np.ascontiguousarray(L0_attn_W_out, dtype=np.float32),
        "lat_k": np.ascontiguousarray(L0_attn_lat_k, dtype=np.float32),
        "lat_v": np.ascontiguousarray(L0_attn_lat_v, dtype=np.float32),
        "W_lk": np.ascontiguousarray(L0_attn_W_lk, dtype=np.float32),
        "W_lv": np.ascontiguousarray(L0_attn_W_lv, dtype=np.float32),
        "Wg": np.ascontiguousarray(L0_ffn_Wg, dtype=np.float32),
        "W11": np.ascontiguousarray(L0_ffn_W11, dtype=np.float32),
        "W12": np.ascontiguousarray(L0_ffn_W12, dtype=np.float32),
        "W2": np.ascontiguousarray(L0_ffn_W2, dtype=np.float32),
    }
    if "nc" not in _NC_CACHE:
        _NC_CACHE["nc"] = build_program()
    nc = _NC_CACHE["nc"]
    in_maps = []
    for core in range(8):
        m = {"x": np.ascontiguousarray(x[2 * core:2 * core + 2])}
        m.update(weights)
        in_maps.append(m)
    res = run_bass_kernel_spmd(nc, in_maps, core_ids=list(range(8)))
    out = np.concatenate([r["y"] for r in res.results], axis=0)
    return out.astype(np.float32)


# version 07
# speedup vs baseline: 1.0306x; 1.0306x over previous
import math
import numpy as np
import concourse.bass as bass
import concourse.mybir as mybir
from concourse.bass_utils import run_bass_kernel_spmd

F32 = mybir.dt.float32
BF16 = mybir.dt.bfloat16
AF = mybir.ActivationFunctionType
ALU = mybir.AluOpType
AX = mybir.AxisListType

BPC, T, C = 2, 2048, 1024
NH, HD, NL = 16, 64, 8
E, FI = 8, 2048
NJ, NU, NTB, NQ, QT = 4, 8, 4, 4, 1024
MAGIC = 12582912.0
LN_TH = math.log(10000.0)
INV2PI = 1.0 / (2.0 * math.pi)
TWO_PI = 2.0 * math.pi
FFN_ON = True
NUE = NU
PH_B = True
PH_C = True
PH_G = True
PH_GATE = True
PH_XW = True
PH_EV2 = True
DBG_DUMP = False


def nsb_of(u):
    return 4 * (u % NJ) + 4


def _cumB(u, h):
    tot = 0
    for uu in range(u):
        tot += 16 * (nsb_of(uu) + 1)
    return tot + h * (nsb_of(u) + 1) if h < 16 else tot + 16 * (nsb_of(u) + 1)


def _cumB16(u, h):
    tot = 0
    for uu in range(u):
        tot += 16 * (nsb_of(uu) + 1)
    return tot + min(h, 16) * (nsb_of(u) + 1)


SC_CUM = EXP_CUM = AV_CUM = _cumB16


def NRM_CUM(u, h):
    return 16 * u + h


def MSK_CUM(u, h):
    return 4 * (16 * u + h)


def build_program():
    nc = bass.Bass("TRN2", target_bir_lowering=False,
                   detect_race_conditions=False)

    x_d = nc.dram_tensor("x", [BPC, T, C], F32, kind="ExternalInput")
    Wq_d = nc.dram_tensor("Wq", [C, C], F32, kind="ExternalInput")
    Wk_d = nc.dram_tensor("Wk", [C, C], F32, kind="ExternalInput")
    Wv_d = nc.dram_tensor("Wv", [C, C], F32, kind="ExternalInput")
    Wout_d = nc.dram_tensor("Wout", [C, C], F32, kind="ExternalInput")
    latk_d = nc.dram_tensor("lat_k", [1, NL, HD], F32, kind="ExternalInput")
    latv_d = nc.dram_tensor("lat_v", [1, NL, HD], F32, kind="ExternalInput")
    Wlk_d = nc.dram_tensor("W_lk", [HD, C], F32, kind="ExternalInput")
    Wlv_d = nc.dram_tensor("W_lv", [HD, C], F32, kind="ExternalInput")
    Wg_d = nc.dram_tensor("Wg", [C, E], F32, kind="ExternalInput")
    W11_d = nc.dram_tensor("W11", [E, C, FI], F32, kind="ExternalInput")
    W12_d = nc.dram_tensor("W12", [E, C, FI], F32, kind="ExternalInput")
    W2_d = nc.dram_tensor("W2", [E, FI, C], F32, kind="ExternalInput")
    y_d = nc.dram_tensor("y", [BPC, T, C], F32, kind="ExternalOutput")
    wstage = nc.dram_tensor("wstage", [4, C, C], BF16, kind="Internal")
    w11s = nc.dram_tensor("w11s", [E, 16, 8, 128, 128], BF16, kind="Internal")
    w12s = nc.dram_tensor("w12s", [E, 16, 8, 128, 128], BF16, kind="Internal")
    w2s = nc.dram_tensor("w2s", [E, 16, 128, C], BF16, kind="Internal")
    xn2s = nc.dram_tensor("xn2s", [8, 128, BPC * T], BF16, kind="Internal")

    def sb(name, shape, dt):
        return nc.alloc_sbuf_tensor(name, shape, dt)

    ident = sb("ident", [128, 128], F32)
    identb = sb("identb", [128, 128], BF16)
    Ctab = sb("Ctab", [128, T], BF16)
    Stab = sb("Stab", [128, T], BF16)
    w_tm = sb("w_tm", [128, 32 * E], F32)
    Wg_sb = sb("Wg_sb", [128, 64], F32)
    ones_f = sb("ones_f", [128, 64], F32)
    ones_b = sb("ones_b", [128, 2], BF16)
    pcol = sb("pcol", [128, 1], F32)
    lncols = sb("lncols", [128, 16], F32)
    frows = sb("frows", [1, 512], F32)
    r_rows = sb("r_rows", [128, 512], F32)
    latkT = sb("latkT", [64, 8], BF16)
    latvT = sb("latvT", [64, 8], BF16)
    lk_fm = sb("lk_fm", [128, 64], BF16)
    lvhat = sb("lvhat", [8, NH * 66], BF16)

    ARENA = 172 * 1024
    arena = sb("arena", [128, ARENA], mybir.dt.uint8)
    abase = nc.lookup_mloc(arena).addr
    _off = [0]

    def at(name, shape, dt, align=64):
        o = (_off[0] + align - 1) // align * align
        nbytes = int(np.prod(shape[1:])) * mybir.dt.size(dt)
        h = nc.alloc_sbuf_tensor_at(name, shape, dt, offset=abase + o)
        _off[0] = o + nbytes
        return h

    slotA = at("slotA", [128, 8 * C], BF16)
    slotB = at("slotB", [128, 8 * C], BF16)
    kbuf = at("kbuf", [128, 8 * T], BF16)
    vhat = at("vhat", [128, 16 * 1056], BF16)
    xn1_fm = at("xn1_fm", [128, 8 * 512], BF16)
    q_fm = at("q_fm", [128, 8 * 512], BF16)
    attn_fm = at("attn_fm", [128, 8 * 512], BF16)
    x_tiles = at("x_tiles", [128, 4 * C], F32)
    Bmask = at("Bmask", [128, 4 * 512], BF16)
    sq_scr = at("sq_scr", [128, C], F32)
    xn2sc = at("xn2sc", [128, C], F32)
    xn1bf = at("xn1bf", [128, C], BF16)
    xn2stage = at("xn2stage", [128, 1024], BF16)
    gl4 = at("gl4", [128, 512], F32)
    rbsb = at("rbsb", [128, 512], F32)
    aof = at("aof", [128, 512], F32)
    vbf = at("vbf", [128, 1024], BF16)
    qbf = at("qbf", [128, 1024], BF16)
    kbf = at("kbf", [128, 1024], BF16)
    qsw = at("qsw", [128, 512], BF16)
    pt0 = at("pt0", [128, 512], BF16)
    pt1 = at("pt1", [128, 512], BF16)
    pt2 = at("pt2", [128, 512], BF16)
    pt3 = at("pt3", [128, 512], BF16)
    ptl = at("ptl", [8, 512], BF16)
    ATTN_END = _off[0]
    assert ATTN_END <= ARENA, f"attn arena {ATTN_END}"

    _off[0] = 0
    xn2_q = at("xn2_q", [128, 8 * 1024], BF16)
    fprm = at("fprm", [128, 16 * 1024], BF16)
    acc = at("acc", [128, 8 * 1024], F32)
    w2_sb0 = at("w2_sb0", [128, 16 * 1024], BF16)
    w2_sb1 = at("w2_sb1", [128, 16 * 1024], BF16)
    w11_sb0 = at("w11_sb0", [128, 8 * 128], BF16)
    w11_sb1 = at("w11_sb1", [128, 8 * 128], BF16)
    w12_sb0 = at("w12_sb0", [128, 8 * 128], BF16)
    w12_sb1 = at("w12_sb1", [128, 8 * 128], BF16)
    sil = at("sil", [128, 1024], F32)
    FFN_END = _off[0]
    assert FFN_END <= ARENA, f"ffn arena {FFN_END}"
    w2_sb = [w2_sb0, w2_sb1]
    w11_sb = [w11_sb0, w11_sb1]
    w12_sb = [w12_sb0, w12_sb1]
    KOFF = 2 * 16384
    grows = nc.alloc_sbuf_tensor_at("grows", [1, T], F32, offset=abase + KOFF)
    rope_u = nc.alloc_sbuf_tensor_at("rope_u", [128, 512], F32,
                                     offset=abase + KOFF + 8192)
    rope_n = nc.alloc_sbuf_tensor_at("rope_n", [128, 512], F32,
                                     offset=abase + KOFF + 8192 + 2048)

    P = [nc.alloc_psum_tensor(f"P{i}", [128, 512], F32) for i in range(7)]
    PT = nc.alloc_psum_tensor("PT", [128, 1024], BF16)

    semnames = [
        "s_x", "s_wqkv", "s_wa", "s_stage", "s_wf", "s_xq", "s_ac", "s_yw",
        "s_xn2w",
        "c_pre",
        "c_vpre", "c_rope", "c_xn1", "c_evA", "c_evVh", "c_msk", "c_recip",
        "c_nrm", "c_res", "c_lnv", "c_xn2", "c_evC2d", "c_gneg", "c_gw",
        "c_fp", "c_dacc",
        "c_spre", "c_lna", "c_evQKV", "c_exp", "c_opev", "c_evC2a", "c_gexp",
        "c_sil",
        "c_sv",
        "c_tpre", "c_qkv", "c_trA", "c_trV", "c_sc", "c_av", "c_rb", "c_opmm",
        "c_trC", "c_trC2", "c_gmm", "c_gg", "c_gv", "c_dmm",
    ]
    S = {n: nc.alloc_semaphore(n) for n in semnames}

    def xt(tb):
        return x_tiles[:, C * tb:C * (tb + 1)]

    def fslc(c):
        return xn1_fm[:, 512 * c:512 * (c + 1)]

    def qslc(hp):
        return q_fm[:, 512 * hp:512 * (hp + 1)]

    def aslc(c):
        return attn_fm[:, 512 * c:512 * (c + 1)]

    def kslc(hp, lo, hi):
        return kbuf[:, T * hp + lo:T * hp + hi]

    def vslc(sb_, h, w):
        base = 1056 * sb_ + 66 * h
        return vhat[:, base:base + w]

    N_WA = 5
    N_STAGE = 4 + (E * 33 if FFN_ON else 0)
    ATT_YW = 16 * NU * NTB

    with nc.Block() as blk:

        @blk.sync
        def _(eng):
            for u in range(NUE):
                sq, t0 = u // NJ, 512 * (u % NJ)
                for tb in range(NTB):
                    if u > 0 and PH_C and PH_G:
                        eng.wait_ge(S["c_xn2"], 4 * (u - 1) + tb + 1)
                        eng.wait_ge(S["s_yw"], 16 * (4 * (u - 1) + tb + 1))
                    eng.dma_start(
                        xt(tb), x_d.ap()[sq, t0 + 128 * tb:t0 + 128 * (tb + 1), :]
                    ).then_inc(S["s_x"], 16)
                eng.wait_ge(S["s_stage"], 16 * 1)
                if u > 0:
                    eng.wait_ge(S["c_qkv"], 24 * u)
                eng.dma_start(slotA[:].rearrange("p (k n) -> p k n", k=8),
                              wstage.ap()[0].rearrange("(k p) n -> p k n", k=8)
                              ).then_inc(S["s_wqkv"], 16)
                eng.wait_ge(S["s_stage"], 16 * 2)
                if u > 0 and PH_C:
                    eng.wait_ge(S["c_opmm"], 8 * u)
                eng.dma_start(slotB[:].rearrange("p (k n) -> p k n", k=8),
                              wstage.ap()[1].rearrange("(k p) n -> p k n", k=8)
                              ).then_inc(S["s_wqkv"], 16)
                eng.wait_ge(S["s_stage"], 16 * 3)
                eng.wait_ge(S["c_qkv"], 24 * u + 8)
                eng.dma_start(slotA[:].rearrange("p (k n) -> p k n", k=8),
                              wstage.ap()[2].rearrange("(k p) n -> p k n", k=8)
                              ).then_inc(S["s_wqkv"], 16)
                eng.wait_ge(S["s_stage"], 16 * 4)
                eng.wait_ge(S["c_qkv"], 24 * u + 16)
                eng.dma_start(slotB[:].rearrange("p (k n) -> p k n", k=8),
                              wstage.ap()[3].rearrange("(k p) n -> p k n", k=8)
                              ).then_inc(S["s_wqkv"], 16)
            if not FFN_ON:
                return
            eng.wait_ge(S["s_yw"], ATT_YW)
            eng.wait_ge(S["s_xn2w"], 16 * 8 * NU)
            eng.wait_ge(S["c_gexp"], 4 * NU)
            eng.wait_ge(S["c_gw"], 4 * NU)
            eng.wait_ge(S["s_stage"], 16 * N_STAGE)
            for q in range(NQ):
                if q > 0:
                    eng.wait_ge(S["c_gv"], 128 * q)
                for c in range(8):
                    eng.dma_start(
                        xn2_q[:, 1024 * c:1024 * (c + 1)],
                        xn2s.ap()[c][:, QT * q:QT * (q + 1)],
                    ).then_inc(S["s_xq"], 16)
                for e in range(E):
                    ge = 8 * q + e
                    if ge >= 2:
                        eng.wait_ge(S["c_dmm"], 8 * (ge - 1))
                    eng.dma_start(
                        w2_sb[e % 2][:].rearrange("p (k h) -> p k h", k=16),
                        w2s.ap()[e].rearrange("k p h -> p k h"),
                    ).then_inc(S["s_wf"], 16)
                    for m in range(16):
                        gvi = 16 * ge + m
                        if gvi >= 2:
                            eng.wait_ge(S["c_gv"], gvi - 1)
                        eng.dma_start(
                            w11_sb[m % 2][:].rearrange("p (k i) -> p k i", k=8),
                            w11s.ap()[e, m].rearrange("k p i -> p k i"),
                        ).then_inc(S["s_wf"], 16)
                        eng.dma_start(
                            w12_sb[m % 2][:].rearrange("p (k i) -> p k i", k=8),
                            w12s.ap()[e, m].rearrange("k p i -> p k i"),
                        ).then_inc(S["s_wf"], 16)

        @blk.gpsimd
        def _(eng):
            IOTA = dict(allow_small_or_imprecise_dtypes=True)
            eng.iota(pcol[:], [[0, 1]], channel_multiplier=1, **IOTA)
            eng.iota(sq_scr[:, 512:640], [[1, 128]], channel_multiplier=0,
                     **IOTA)
            for r in range(4):
                eng.iota(sq_scr[:, 0:512].rearrange("p (a b) -> p a b",
                                                    a=32, b=16),
                         [[16, 32], [0, 16]], base=16 - 128 * r,
                         channel_multiplier=0, **IOTA)
                eng.iota(lncols[:, 8 + r:9 + r], [[0, 1]],
                         channel_multiplier=1, **IOTA
                         ).then_inc(S["c_pre"], 1)
                eng.wait_ge(S["c_vpre"], r + 1)
            eng.iota(frows[0:1, 0:128].rearrange("p (a i r) -> p a i r",
                                                 a=2, i=32, r=2),
                     [[0, 2], [1, 32], [0, 2]], channel_multiplier=0, **IOTA)
            eng.iota(frows[0:1, 128:256].rearrange("p (a i r) -> p a i r",
                                                   a=2, i=32, r=2),
                     [[0, 2], [0, 32], [1, 2]], channel_multiplier=0, **IOTA)
            eng.iota(grows[:], [[1, T]], channel_multiplier=0, **IOTA)
            eng.memset(ones_f[:], 1.0)
            eng.memset(ones_b[:], 1.0)
            eng.memset(vhat[:], 0.0)
            eng.memset(lvhat[:], 0.0)
            eng.memset(vhat[:].rearrange("p (s h d) -> p s h d",
                                         s=16, h=16, d=66)[:, :, :, 64:65], 1.0)
            eng.memset(lvhat[:].rearrange("p (h d) -> p h d",
                                          h=16, d=66)[:, :, 64:65], 1.0
                       ).then_inc(S["c_pre"], 1)
            eng.dma_start(xn1_fm[0:64, 0:1024], Wlk_d.ap()
                          ).then_inc(S["s_wa"], 16)
            eng.dma_start(xn1_fm[0:64, 1024:2048], Wlv_d.ap()
                          ).then_inc(S["s_wa"], 16)
            eng.dma_start(sq_scr[0:8, 640:704], latk_d.ap()[0]
                          ).then_inc(S["s_wa"], 16)
            eng.dma_start(sq_scr[0:8, 704:768], latv_d.ap()[0]
                          ).then_inc(S["s_wa"], 16)
            eng.dma_start(Wg_sb[:].rearrange("p (k e) -> p k e", k=8),
                          Wg_d.ap().rearrange("(k p) e -> p k e", k=8)
                          ).then_inc(S["s_wa"], 16)
            for i, wd in enumerate([Wq_d, Wk_d, Wv_d, Wout_d]):
                eng.dma_start(wstage.ap()[i], wd.ap()).then_inc(S["s_stage"], 16)
            if FFN_ON:
                for e in range(E):
                    eng.dma_start(w2s.ap()[e].rearrange("k p h -> (k p) h"),
                                  W2_d.ap()[e]).then_inc(S["s_stage"], 16)
                    for m in range(16):
                        eng.dma_start(
                            w11s.ap()[e, m].rearrange("k p i -> (k p) i"),
                            W11_d.ap()[e][:, 128 * m:128 * (m + 1)],
                        ).then_inc(S["s_stage"], 16)
                        eng.dma_start(
                            w12s.ap()[e, m].rearrange("k p i -> (k p) i"),
                            W12_d.ap()[e][:, 128 * m:128 * (m + 1)],
                        ).then_inc(S["s_stage"], 16)
            for u in range(NUE if PH_C else 0):
                sq, t0 = u // NJ, 512 * (u % NJ)
                eng.wait_ge(S["c_res"], 8 * (u + 1))
                for tb in range(NTB):
                    eng.dma_start(
                        y_d.ap()[sq, t0 + 128 * tb:t0 + 128 * (tb + 1), :],
                        xt(tb)).then_inc(S["s_yw"], 16)
                for tb in range(NTB if (PH_G and PH_XW) else 0):
                    for cb in range(2):
                        bidx = 2 * (4 * u + tb) + cb
                        eng.wait_ge(S["c_evC2d"], bidx + 1)
                        slot = xn2stage[:, 512 * (bidx % 2):512 * (bidx % 2) + 512]
                        gtb = 4 * u + tb
                        dst = xn2s.ap()[4 * cb:4 * cb + 4][:, :,
                                                           128 * gtb:128 * (gtb + 1)
                                                           ].rearrange(
                            "c p t -> p c t")
                        eng.dma_start(
                            dst, slot.rearrange("p (c t) -> p c t", c=4)
                        ).then_inc(S["s_xn2w"], 16)
            if DBG_DUMP == 2:
                eng.wait_ge(S["c_gw"], 32)
                eng.dma_start(y_d.ap()[1, 0:128, 0:256], w_tm[:]
                              ).then_inc(S["s_yw"], 16)
                eng.wait_ge(S["s_xn2w"], 16 * 64)
                eng.dma_start(y_d.ap()[1, 128:256, :],
                              xn2s.ap()[0][:, 0:1024]).then_inc(S["s_yw"], 16)
                eng.dma_start(y_d.ap()[1, 256:384, :],
                              xn2s.ap()[5][:, 1024:2048]).then_inc(S["s_yw"], 16)
                eng.wait_ge(S["s_yw"], 16 * (32 + 3))
                return
            if DBG_DUMP:
                if PH_C:
                    eng.wait_ge(S["c_res"], 8)
                    eng.dma_start(y_d.ap()[1, 0:128, :], x_tiles[:, 0:1024]
                                  ).then_inc(S["s_yw"], 16)
                    eng.dma_start(y_d.ap()[1, 128:256, 0:512], aof[:, 0:512]
                                  ).then_inc(S["s_yw"], 16)
                eng.wait_ge(S["c_nrm"], 16)
                eng.dma_start(y_d.ap()[0, 0:128, :], attn_fm[:, 0:1024]
                              ).then_inc(S["s_yw"], 16)
                eng.dma_start(y_d.ap()[0, 128:256, :], q_fm[:, 0:1024]
                              ).then_inc(S["s_yw"], 16)
                eng.dma_start(y_d.ap()[0, 256:384, :], xn1_fm[:, 0:1024]
                              ).then_inc(S["s_yw"], 16)
                eng.dma_start(y_d.ap()[0, 384:512, 0:512], kbuf[:, 0:512]
                              ).then_inc(S["s_yw"], 16)
                eng.dma_start(y_d.ap()[0, 512:640, :], vhat[:, 0:1024]
                              ).then_inc(S["s_yw"], 16)
                eng.dma_start(y_d.ap()[0, 640:768, :], Ctab[:, 0:1024]
                              ).then_inc(S["s_yw"], 16)
                eng.wait_ge(S["s_yw"], 16 * (8 if PH_C else 6))
                return
            if FFN_ON:
                for q in range(NQ):
                    sq, t0 = q // 2, QT * (q % 2)
                    for tb in range(8):
                        eng.wait_ge(S["s_yw"], ATT_YW)
                        if q > 0:
                            eng.wait_ge(S["s_yw"],
                                        ATT_YW + 16 * (8 * (q - 1) + tb + 1))
                        eng.dma_start(
                            acc[:, 1024 * tb:1024 * (tb + 1)],
                            y_d.ap()[sq, t0 + 128 * tb:t0 + 128 * (tb + 1), :],
                        ).then_inc(S["s_ac"], 16)
                    for tb in range(8):
                        eng.wait_ge(S["c_dacc"], 64 * q + 56 + tb + 1)
                        eng.dma_start(
                            y_d.ap()[sq, t0 + 128 * tb:t0 + 128 * (tb + 1), :],
                            acc[:, 1024 * tb:1024 * (tb + 1)],
                        ).then_inc(S["s_yw"], 16)
                eng.wait_ge(S["s_yw"], ATT_YW + 16 * 8 * NQ)
            else:
                eng.wait_ge(S["s_yw"], 16 * 4 * NUE if PH_C else 0)

        @blk.tensor
        def _(eng):
            eng.wait_ge(S["s_wa"], 16 * N_WA)
            eng.wait_ge(S["c_pre"], 5)
            eng.wait_ge(S["c_vpre"], 5)
            eng.transpose(P[6][0:64, 0:8], sq_scr[0:8, 640:704],
                          ident[0:8, 0:8])
            eng.transpose(P[6][0:64, 8:16], sq_scr[0:8, 704:768],
                          ident[0:8, 0:8]).then_inc(S["c_tpre"], 1)
            eng.wait_ge(S["c_vpre"], 7)
            for hp in range(8):
                eng.matmul(P[0][:, 8 * hp:8 * hp + 8],
                           xn1_fm[0:64, 128 * hp:128 * (hp + 1)], latkT[:, :],
                           start=True, stop=True)
            eng.matmul(P[1][0:8, 0:512], latvT[:, :], xn1_fm[0:64, 1024:1536],
                       start=True, stop=True)
            eng.matmul(P[2][0:8, 0:512], latvT[:, :], xn1_fm[0:64, 1536:2048],
                       start=True, stop=True).then_inc(S["c_tpre"], 1)
            eng.wait_ge(S["c_vpre"], 6)
            for idx in range(8):
                eng.wait_ge(S["c_rope"], idx)
                s_, w_ = idx // 2, idx % 2
                eng.matmul(P[3][:, :],
                           frows[0:1, 256 * w_:256 * w_ + 128],
                           grows[:, 512 * s_:512 * (s_ + 1)],
                           start=True, stop=True).then_inc(S["c_tpre"], 1)

            for u in range(NUE):
                j = u % NJ
                nsb = nsb_of(u)
                for tb in range(NTB):
                    idx = 4 * u + tb
                    eng.wait_ge(S["c_xn1"], idx + 1)
                    for half in range(2):
                        bidx = 2 * idx + half
                        if bidx > 0:
                            eng.wait_ge(S["c_evA"], bidx)
                        if u > 0 and tb == 0 and half == 0:
                            eng.wait_ge(S["c_evVh"], 8 * u)
                        last = None
                        for cc in range(4):
                            c = 4 * half + cc
                            last = eng.transpose(
                                PT[:, 128 * cc:128 * (cc + 1)],
                                xn1bf[:, 128 * c:128 * (c + 1)], identb[:, :])
                        last.then_inc(S["c_trA"], 1)
                for gi in range(24):
                    proj, hp = gi // 8, gi % 8
                    gno = 24 * u + gi
                    bank = P[gi % 3]
                    eng.wait_ge(S["s_wqkv"], 16 * (4 * u + min(proj + 1, 3)))
                    if gi == 0:
                        eng.wait_ge(S["c_evA"], 8 * (u + 1))
                        if u == 0:
                            eng.wait_ge(S["c_vpre"], 8)
                    if gi == 2 and u > 0:
                        eng.wait_ge(S["c_gexp"], 4 * u)
                    if gno >= 3:
                        eng.wait_ge(S["c_evQKV"], gno - 2)
                    wsl = slotA if proj in (0, 2) else slotB
                    last = None
                    for kk in range(8):
                        last = eng.matmul(
                            bank[:, :],
                            wsl[:, 1024 * kk + 128 * hp:
                                1024 * kk + 128 * hp + 128],
                            fslc(kk), start=(kk == 0), stop=(kk == 7))
                    last.then_inc(S["c_qkv"], 1)

                    def _vtr(hp_):
                        i2 = 8 * u + hp_
                        eng.wait_ge(S["c_evQKV"], 24 * u + 16 + hp_ + 1)
                        if i2 > 0:
                            eng.wait_ge(S["c_evVh"], i2)
                        vof = 512 * (hp_ % 2)
                        last = None
                        for sb_ in range(4):
                            last = eng.transpose(
                                PT[:, 128 * sb_:128 * (sb_ + 1)],
                                vbf[:, vof + 128 * sb_:vof + 128 * (sb_ + 1)],
                                identb[:, :])
                        last.then_inc(S["c_trV"], 1)
                    if proj == 2 and hp >= 1:
                        _vtr(hp - 1)
                    if proj == 2 and hp == 7:
                        _vtr(7)
                for h in range(16 if PH_B else 0):
                    hp, hh = h // 2, h % 2
                    row0 = 64 * hh
                    tp = (64, 0) if hh else None
                    eng.wait_ge(S["c_rope"], 8 + 16 * u + 8 + hp + 1)
                    eng.wait_ge(S["c_nrm"], NRM_CUM(u, h) if hh == 0
                                else max(NRM_CUM(u, h) - 1, 0))
                    eng.wait_ge(S["c_evVh"], 8 * u + hp + 1)
                    eng.matmul(P[6][0:8, :],
                               lk_fm[row0:row0 + 64, 8 * hp:8 * hp + 8],
                               qslc(hp)[row0:row0 + 64, :],
                               start=True, stop=True,
                               tile_position=tp).then_inc(S["c_sc"], 1)
                    def _score(sb_):
                        if sb_ >= 3:
                            eng.wait_ge(S["c_exp"], EXP_CUM(u, h) + sb_ - 1)
                        eng.matmul(P[(4, 5, 0)[sb_ % 3]][:, :],
                                   kslc(hp, 128 * sb_, 128 * (sb_ + 1))
                                   [row0:row0 + 64, :],
                                   qslc(hp)[row0:row0 + 64, :],
                                   start=True, stop=True,
                                   tile_position=tp).then_inc(S["c_sc"], 1)
                    if hh == 1:
                        eng.wait_ge(S["c_exp"], EXP_CUM(u, h) + 1)

                    def _av(sb_):
                        pt = (pt0, pt1, pt2, pt3)[sb_ % 4]
                        if hh == 1:
                            eng.matmul(P[6][32:33, :], ones_b[:, 0:1],
                                       pt[:, :], start=(sb_ == 0), stop=False,
                                       tile_position=(0, 32))
                            eng.matmul(P[6][64:128, :], vslc(sb_, h, 64),
                                       pt[:, :], start=(sb_ == 0), stop=False,
                                       tile_position=(0, 64)
                                       ).then_inc(S["c_av"], 1)
                        else:
                            eng.matmul(P[2][0:66, :], vslc(sb_, h, 66),
                                       pt[:, :], start=(sb_ == 0), stop=False
                                       ).then_inc(S["c_av"], 1)

                    def _ptwait(sb_):
                        if sb_ < 4 * j:
                            eng.wait_ge(S["c_exp"], EXP_CUM(u, h) + sb_ + 2)
                        else:
                            eng.wait_ge(S["c_msk"],
                                        MSK_CUM(u, h) + (sb_ - 4 * j) + 1)

                    eng.wait_ge(S["c_exp"], EXP_CUM(u, h))
                    for q_ in range(min(3, nsb)):
                        _score(q_)
                    sb_ = 0
                    while sb_ < nsb:
                        hi = min(sb_ + 3, nsb)
                        for cur in range(sb_, hi):
                            _ptwait(cur)
                        for cur in range(sb_ + 3, min(sb_ + 6, nsb)):
                            _score(cur)
                        for cur in range(sb_, hi):
                            _av(cur)
                        sb_ += 3
                    eng.wait_ge(S["c_exp"], EXP_CUM(u, h) + 1)
                    if hh == 1:
                        eng.matmul(P[6][32:33, :], ones_b[0:8, 0:1], ptl[:, :],
                                   start=False, stop=True,
                                   tile_position=(0, 32))
                        eng.matmul(P[6][64:128, :],
                                   lvhat[:, 66 * h:66 * h + 64], ptl[:, :],
                                   start=False, stop=True,
                                   tile_position=(0, 64)).then_inc(S["c_av"], 1)
                    else:
                        eng.matmul(P[2][0:66, :], lvhat[:, 66 * h:66 * h + 66],
                                   ptl[:, :], start=False, stop=True
                                   ).then_inc(S["c_av"], 1)
                    eng.wait_ge(S["c_recip"], NRM_CUM(u, h) + 1)
                    if hh == 0:
                        eng.matmul(P[3][0:64, :], ones_f[64:65, 0:64],
                                   r_rows[64:65, :], start=True, stop=True,
                                   tile_position=(64, 0)).then_inc(S["c_rb"], 1)
                    else:
                        eng.matmul(P[1][64:128, :], ones_f[32:33, 0:64],
                                   r_rows[32:33, :], start=True, stop=True,
                                   tile_position=(32, 64)).then_inc(S["c_rb"], 1)
                for c in range(8 if PH_C else 0):
                    oidx = 8 * u + c
                    eng.wait_ge(S["s_wqkv"], 16 * (4 * u + 4))
                    if c == 0:
                        eng.wait_ge(S["c_exp"], EXP_CUM(u, 16))
                        eng.wait_ge(S["c_nrm"], 16 * (u + 1))
                        eng.wait_ge(S["c_recip"], 16 * (u + 1))
                    if oidx >= 2:
                        eng.wait_ge(S["c_opev"], oidx - 1)
                    if c == 0:
                        eng.wait_ge(S["c_evC2d"], 8 * u)
                        eng.wait_ge(S["c_evC2a"], 8 * u)
                    last = None
                    for kk in range(8):
                        last = eng.matmul(
                            P[oidx % 2][:, :],
                            slotB[:, 1024 * kk + 128 * c:
                                  1024 * kk + 128 * c + 128],
                            aslc(kk), start=(kk == 0), stop=(kk == 7))
                    last.then_inc(S["c_opmm"], 1)
                    eng.wait_ge(S["c_opev"], oidx + 1)
                    if oidx > 0:
                        eng.wait_ge(S["c_res"], oidx)
                    last = None
                    for tb in range(NTB):
                        last = eng.transpose(
                            P[6][:, 128 * tb:128 * (tb + 1)],
                            aof[:, 128 * tb:128 * (tb + 1)], ident[:, :])
                    last.then_inc(S["c_trC"], 1)
                for tb in range(NTB if (PH_C and PH_G) else 0):
                    idx = 4 * u + tb
                    for half in range(2):
                        bidx = 2 * idx + half
                        eng.wait_ge(S["c_xn2"], idx + 1)
                        if tb == 0 and half == 0:
                            eng.wait_ge(S["c_res"], 8 * (u + 1))
                        if bidx > 0:
                            eng.wait_ge(S["c_evC2d"], bidx)
                            eng.wait_ge(S["c_evC2a"], bidx)
                        last = None
                        for cc in range(4):
                            c = 4 * half + cc
                            last = eng.transpose(
                                P[6][:, 128 * cc:128 * (cc + 1)],
                                xn2sc[:, 128 * c:128 * (c + 1)], ident[:, :])
                        last.then_inc(S["c_trC2"], 1)
                        if not PH_GATE:
                            continue
                        eng.wait_ge(S["c_evC2a"], bidx + 1)
                        if tb == 0 and half == 0:
                            eng.wait_ge(S["c_nrm"], 16 * (u + 1))
                            eng.wait_ge(S["c_recip"], 16 * (u + 1))
                        if half == 0 and idx > 0:
                            eng.wait_ge(S["c_gexp"], idx)
                        last = None
                        for cc in range(4):
                            c = 4 * half + cc
                            last = eng.matmul(
                                P[2][:, 0:8], gl4[:, 128 * cc:128 * (cc + 1)],
                                Wg_sb[:, 8 * c:8 * (c + 1)],
                                start=(c == 0), stop=(c == 7))
                        last.then_inc(S["c_gmm"], 1)

            if not FFN_ON:
                return
            eng.wait_ge(S["c_gexp"], 4 * NU)
            eng.wait_ge(S["c_nrm"], 16 * NU)
            eng.wait_ge(S["c_opev"], 8 * NU)
            eng.wait_ge(S["c_evC2a"], 8 * NU)
            eng.wait_ge(S["c_evC2d"], 8 * NU)
            eng.wait_ge(S["c_res"], 8 * NU)
            for q in range(NQ):
                for e in range(E):
                    ge = 8 * q + e
                    for m in range(16):
                        gvi = 16 * ge + m
                        pgs = (P[0], P[1]) if m % 2 == 0 else (P[2], P[3])
                        wb11, wb12 = w11_sb[m % 2], w12_sb[m % 2]
                        eng.wait_ge(S["s_wf"], 16 * (33 * ge + 1 + 2 * (m + 1)))
                        if m == 0:
                            eng.wait_ge(S["s_xq"], 16 * 8 * (q + 1))
                        if gvi >= 2:
                            eng.wait_ge(S["c_sil"], gvi - 1)
                        if m == 0 and ge > 0:
                            eng.wait_ge(S["c_dacc"], 8 * ge)
                        last = None
                        for kk in range(8):
                            st_, sp_ = (kk == 0), (kk == 7)
                            eng.matmul(pgs[0][:, :],
                                       wb11[:, 128 * kk:128 * (kk + 1)],
                                       xn2_q[:, 1024 * kk:1024 * kk + 512],
                                       start=st_, stop=sp_)
                            last = eng.matmul(
                                pgs[1][:, :], wb11[:, 128 * kk:128 * (kk + 1)],
                                xn2_q[:, 1024 * kk + 512:1024 * (kk + 1)],
                                start=st_, stop=sp_)
                        last.then_inc(S["c_gg"], 1)
                        if gvi >= 1:
                            eng.wait_ge(S["c_fp"], gvi)
                        last = None
                        for kk in range(8):
                            st_, sp_ = (kk == 0), (kk == 7)
                            eng.matmul(P[4][:, :],
                                       wb12[:, 128 * kk:128 * (kk + 1)],
                                       xn2_q[:, 1024 * kk:1024 * kk + 512],
                                       start=st_, stop=sp_)
                            last = eng.matmul(
                                P[5][:, :], wb12[:, 128 * kk:128 * (kk + 1)],
                                xn2_q[:, 1024 * kk + 512:1024 * (kk + 1)],
                                start=st_, stop=sp_)
                        last.then_inc(S["c_gv"], 1)
                    eng.wait_ge(S["s_wf"], 16 * (33 * ge + 1))
                    eng.wait_ge(S["c_fp"], 16 * (ge + 1))
                    wb2 = w2_sb[e % 2]
                    for tb in range(8):
                        di = 8 * ge + tb
                        pds = (P[0], P[1]) if tb % 2 == 0 else (P[2], P[3])
                        if tb >= 2:
                            eng.wait_ge(S["c_dacc"], di - 1)
                        last = None
                        for kk in range(16):
                            st_, sp_ = (kk == 0), (kk == 15)
                            eng.matmul(pds[0][:, :],
                                       fprm[:, 1024 * kk + 128 * tb:
                                            1024 * kk + 128 * tb + 128],
                                       wb2[:, 1024 * kk:1024 * kk + 512],
                                       start=st_, stop=sp_)
                            last = eng.matmul(
                                pds[1][:, :],
                                fprm[:, 1024 * kk + 128 * tb:
                                     1024 * kk + 128 * tb + 128],
                                wb2[:, 1024 * kk + 512:1024 * (kk + 1)],
                                start=st_, stop=sp_)
                        last.then_inc(S["c_dmm"], 1)

        @blk.vector
        def _(eng):
            sv = [0]
            for r in range(4):
                eng.wait_ge(S["c_pre"], r + 1)
                eng.tensor_scalar(Bmask[:, 512 * r:512 * (r + 1)],
                                  sq_scr[:, 0:512], lncols[:, 8 + r:9 + r],
                                  None, ALU.is_gt).then_inc(S["c_vpre"], 1)
            eng.tensor_scalar(ident[:], sq_scr[:, 512:640], pcol[:], None,
                              ALU.is_equal).then_inc(S["c_sv"], 1)
            sv[0] += 1
            eng.wait_ge(S["c_sv"], sv[0])
            eng.tensor_copy(identb[:], ident[:]).then_inc(S["c_vpre"], 1)
            eng.wait_ge(S["c_spre"], 1)
            eng.tensor_scalar(frows[0:1, 128:256], frows[0:1, 128:256], 2.0,
                              -1.0, ALU.mult, ALU.add).then_inc(S["c_sv"], 1)
            sv[0] += 1
            eng.wait_ge(S["c_sv"], sv[0])
            eng.tensor_mul(frows[0:1, 256:384], frows[0:1, 0:128],
                           frows[0:1, 128:256]).then_inc(S["c_sv"], 1)
            sv[0] += 1
            eng.wait_ge(S["c_sv"], sv[0])
            eng.tensor_scalar(frows[0:1, 0:128], frows[0:1, 0:128], INV2PI,
                              None, ALU.mult)
            eng.tensor_scalar(frows[0:1, 256:384], frows[0:1, 256:384],
                              INV2PI, None, ALU.mult).then_inc(S["c_vpre"], 1)
            eng.wait_ge(S["c_tpre"], 1)
            eng.tensor_copy(latkT[:], P[6][0:64, 0:8])
            eng.tensor_copy(latvT[:], P[6][0:64, 8:16]).then_inc(S["c_vpre"], 1)
            eng.wait_ge(S["c_tpre"], 2)
            eng.tensor_copy(lk_fm[:], P[0][:, 0:64])
            eng.tensor_copy(
                lvhat[:].rearrange("p (h d) -> p h d", h=16, d=66)
                [:, 0:8, 0:64],
                P[1][0:8, 0:512].rearrange("p (h d) -> p h d", h=8, d=64))
            eng.tensor_copy(
                lvhat[:].rearrange("p (h d) -> p h d", h=16, d=66)
                [:, 8:16, 0:64],
                P[2][0:8, 0:512].rearrange("p (h d) -> p h d", h=8, d=64)
            ).then_inc(S["c_vpre"], 1)
            for idx in range(8):
                eng.wait_ge(S["c_tpre"], 2 + idx + 1)
                if idx > 0:
                    eng.wait_ge(S["c_spre"], 1 + idx)
                w_ = idx % 2
                if w_ == 0:
                    last = eng.tensor_scalar(rope_u[:], P[3][:, :], 0.25, None,
                                             ALU.add)
                else:
                    last = eng.tensor_copy(rope_u[:], P[3][:, :])
                last.then_inc(S["c_sv"], 1)
                sv[0] += 1
                eng.wait_ge(S["c_sv"], sv[0])
                eng.tensor_scalar(rope_n[:], rope_u[:], MAGIC, MAGIC,
                                  ALU.add, ALU.subtract).then_inc(S["c_sv"], 1)
                sv[0] += 1
                eng.wait_ge(S["c_sv"], sv[0])
                eng.scalar_tensor_tensor(rope_u[:], rope_n[:], -1.0,
                                         rope_u[:], ALU.mult, ALU.add
                                         ).then_inc(S["c_rope"], 1)

            mask32 = [i ^ 1 for i in range(32)]
            for u in range(NUE):
                j = u % NJ
                nsb = nsb_of(u)
                for tb in range(NTB):
                    idx = 4 * u + tb
                    eng.wait_ge(S["s_x"], 16 * (idx + 1))
                    if idx == 0:
                        eng.wait_ge(S["c_tpre"], 1)
                    x_ = xt(tb)
                    def _sg(ins_):
                        ins_.then_inc(S["c_sv"], 1)
                        sv[0] += 1
                        eng.wait_ge(S["c_sv"], sv[0])
                    _sg(eng.reduce_sum(lncols[:, 0:1], x_, axis=AX.X))
                    _sg(eng.tensor_mul(sq_scr[:], x_, x_))
                    _sg(eng.reduce_sum(lncols[:, 1:2], sq_scr[:], axis=AX.X))
                    _sg(eng.tensor_scalar(lncols[:, 2:3], lncols[:, 0:1],
                                          1.0 / C, None, ALU.mult))
                    _sg(eng.tensor_scalar(lncols[:, 3:4], lncols[:, 1:2],
                                          1.0 / C, None, ALU.mult))
                    _sg(eng.tensor_scalar(lncols[:, 4:5], lncols[:, 2:3],
                                          lncols[:, 2:3], None, ALU.mult))
                    eng.tensor_scalar(lncols[:, 5:6], lncols[:, 3:4],
                                      lncols[:, 4:5], 1e-5, ALU.subtract,
                                      ALU.add).then_inc(S["c_lnv"], 1)
                    eng.wait_ge(S["c_lna"], 8 * u + tb + 1)
                    eng.reciprocal(lncols[:, 7:8], lncols[:, 6:7]
                                   ).then_inc(S["c_sv"], 1)
                    sv[0] += 1
                    eng.wait_ge(S["c_sv"], sv[0])
                    if idx > 0:
                        eng.wait_ge(S["c_trA"], 2 * idx)
                    eng.tensor_scalar(xn1bf[:], x_, lncols[:, 2:3],
                                      lncols[:, 7:8], ALU.subtract, ALU.mult
                                      ).then_inc(S["c_xn1"], 1)
                    for half in range(2):
                        bidx = 2 * idx + half
                        eng.wait_ge(S["c_trA"], bidx + 1)
                        if bidx == 0:
                            eng.wait_ge(S["c_tpre"], 2)
                        if u > 0 and tb == 0 and half == 0:
                            eng.wait_ge(S["c_qkv"], 24 * u)
                        last = None
                        for cc in range(4):
                            c = 4 * half + cc
                            last = eng.tensor_copy(
                                fslc(c)[:, 128 * tb:128 * (tb + 1)],
                                PT[:, 128 * cc:128 * (cc + 1)])
                        last.then_inc(S["c_evA"], 1)
                for hp in range(8):
                    eng.wait_ge(S["c_evQKV"], 24 * u + hp + 1)
                    if u == 0 and hp == 0:
                        eng.wait_ge(S["c_spre"], 9)
                    if u > 0:
                        eng.wait_ge(S["c_sc"], SC_CUM(u - 1, 2 * hp + 2))
                    qsl = qbf[:, 512 * (hp % 2):512 * (hp % 2) + 512]
                    eng.stream_shuffle(qsw[:], qsl, mask32
                                       ).then_inc(S["c_sv"], 1)
                    sv[0] += 1
                    eng.wait_ge(S["c_sv"], sv[0])
                    eng.tensor_mul(qsw[:], qsw[:], Stab[:, 512 * j:512 * (j + 1)])
                    eng.tensor_mul(qsl, qsl, Ctab[:, 512 * j:512 * (j + 1)]
                                   ).then_inc(S["c_sv"], 1)
                    sv[0] += 1
                    eng.wait_ge(S["c_sv"], sv[0])
                    eng.tensor_add(qslc(hp), qsl, qsw[:]
                                   ).then_inc(S["c_rope"], 1)
                for hp in range(8):
                    eng.wait_ge(S["c_evQKV"], 24 * u + 8 + hp + 1)
                    if u >= 4 and hp == 0:
                        eng.wait_ge(S["c_sc"], SC_CUM(4, 0))
                    ksl = kbf[:, 512 * (hp % 2):512 * (hp % 2) + 512]
                    eng.stream_shuffle(qsw[:], ksl, mask32
                                       ).then_inc(S["c_sv"], 1)
                    sv[0] += 1
                    eng.wait_ge(S["c_sv"], sv[0])
                    eng.tensor_mul(qsw[:], qsw[:], Stab[:, 512 * j:512 * (j + 1)])
                    eng.tensor_mul(ksl, ksl, Ctab[:, 512 * j:512 * (j + 1)]
                                   ).then_inc(S["c_sv"], 1)
                    sv[0] += 1
                    eng.wait_ge(S["c_sv"], sv[0])
                    eng.tensor_add(kslc(hp, 512 * j, 512 * (j + 1)),
                                   ksl, qsw[:]).then_inc(S["c_rope"], 1)
                for hp in range(8):
                    i2 = 8 * u + hp
                    eng.wait_ge(S["c_trV"], i2 + 1)
                    if u >= 4 and hp == 0:
                        eng.wait_ge(S["c_av"], AV_CUM(4, 0))
                    last = None
                    for sb_ in range(4):
                        vsrc = PT[:, 128 * sb_:128 * (sb_ + 1)].rearrange(
                            "p (h d) -> p h d", h=2, d=64)
                        base = 1056 * (4 * j + sb_) + 132 * hp
                        vdst = vhat[:, base:base + 132].rearrange(
                            "p (h d) -> p h d", h=2, d=66)[:, :, 0:64]
                        last = eng.tensor_copy(vdst, vsrc)
                    last.then_inc(S["c_evVh"], 1)
                for h in range(16 if PH_B else 0):
                    hp, hh = h // 2, h % 2
                    for r in range(4):
                        sb_ = 4 * j + r
                        eng.wait_ge(S["c_exp"], EXP_CUM(u, h) + sb_ + 2)
                        pt = (pt0, pt1, pt2, pt3)[sb_ % 4]
                        eng.tensor_mul(pt[:, :], pt[:, :],
                                       Bmask[:, 512 * r:512 * (r + 1)]
                                       ).then_inc(S["c_msk"], 1)
                    eng.wait_ge(S["c_av"], AV_CUM(u, h) + nsb + 1)
                    if hh == 0:
                        eng.reciprocal(r_rows[64:65, :], P[2][64:65, :]
                                       ).then_inc(S["c_recip"], 1)
                    else:
                        eng.reciprocal(r_rows[32:33, :], P[6][32:33, :]
                                       ).then_inc(S["c_recip"], 1)
                    eng.wait_ge(S["c_rb"], NRM_CUM(u, h) + 1)
                    if PH_C:
                        eng.wait_ge(S["c_opmm"], 8 * u)
                    if hh == 0:
                        eng.tensor_copy(rbsb[0:64, :], P[3][0:64, :]
                                        ).then_inc(S["c_sv"], 1)
                        sv[0] += 1
                        eng.wait_ge(S["c_sv"], sv[0])
                        eng.tensor_mul(aslc(hp)[0:64, :], P[2][0:64, :],
                                       rbsb[0:64, :]).then_inc(S["c_nrm"], 1)
                    else:
                        eng.tensor_copy(rbsb[64:128, :], P[1][64:128, :]
                                        ).then_inc(S["c_sv"], 1)
                        sv[0] += 1
                        eng.wait_ge(S["c_sv"], sv[0])
                        eng.tensor_mul(aslc(hp)[64:128, :], P[6][64:128, :],
                                       rbsb[64:128, :]).then_inc(S["c_nrm"], 1)
                for c in range(8 if PH_C else 0):
                    oidx = 8 * u + c
                    eng.wait_ge(S["c_trC"], oidx + 1)
                    last = None
                    for tb in range(NTB):
                        last = eng.scalar_tensor_tensor(
                            xt(tb)[:, 128 * c:128 * (c + 1)],
                            P[6][:, 128 * tb:128 * (tb + 1)], 1.0,
                            xt(tb)[:, 128 * c:128 * (c + 1)],
                            ALU.mult, ALU.add)
                    last.then_inc(S["c_res"], 1)
                for tb in range(NTB if (PH_C and PH_G) else 0):
                    idx = 4 * u + tb
                    x_ = xt(tb)
                    def _sg(ins_):
                        ins_.then_inc(S["c_sv"], 1)
                        sv[0] += 1
                        eng.wait_ge(S["c_sv"], sv[0])
                    _sg(eng.reduce_sum(lncols[:, 0:1], x_, axis=AX.X))
                    _sg(eng.tensor_mul(sq_scr[:], x_, x_))
                    _sg(eng.reduce_sum(lncols[:, 1:2], sq_scr[:], axis=AX.X))
                    _sg(eng.tensor_scalar(lncols[:, 2:3], lncols[:, 0:1],
                                          1.0 / C, None, ALU.mult))
                    _sg(eng.tensor_scalar(lncols[:, 3:4], lncols[:, 1:2],
                                          1.0 / C, None, ALU.mult))
                    _sg(eng.tensor_scalar(lncols[:, 4:5], lncols[:, 2:3],
                                          lncols[:, 2:3], None, ALU.mult))
                    eng.tensor_scalar(lncols[:, 5:6], lncols[:, 3:4],
                                      lncols[:, 4:5], 1e-5, ALU.subtract,
                                      ALU.add).then_inc(S["c_lnv"], 1)
                    eng.wait_ge(S["c_lna"], 8 * u + 4 + tb + 1)
                    eng.reciprocal(lncols[:, 7:8], lncols[:, 6:7]
                                   ).then_inc(S["c_sv"], 1)
                    sv[0] += 1
                    eng.wait_ge(S["c_sv"], sv[0])
                    if idx > 0:
                        eng.wait_ge(S["c_trC2"], 2 * idx)
                    eng.tensor_scalar(xn2sc[:], x_, lncols[:, 2:3],
                                      lncols[:, 7:8], ALU.subtract, ALU.mult
                                      ).then_inc(S["c_xn2"], 1)
                    for half in range(2):
                        bidx = 2 * idx + half
                        eng.wait_ge(S["c_trC2"], bidx + 1)
                        if PH_XW and bidx >= 2:
                            eng.wait_ge(S["s_xn2w"], 16 * (bidx - 1))
                        slot = xn2stage[:, 512 * (bidx % 2):
                                        512 * (bidx % 2) + 512]
                        last = None
                        if PH_EV2:
                            for cc in range(4):
                                last = eng.tensor_copy(
                                    slot[:, 128 * cc:128 * (cc + 1)],
                                    P[6][:, 128 * cc:128 * (cc + 1)])
                            last.then_inc(S["c_evC2d"], 1)
                        else:
                            eng.engine_nop().then_inc(S["c_evC2d"], 1)
                    if not PH_GATE:
                        continue
                    eng.wait_ge(S["c_gmm"], 2 * (idx + 1))
                    eng.reduce_max(lncols[:, 9:10], P[2][:, 0:8], axis=AX.X
                                   ).then_inc(S["c_sv"], 1)
                    sv[0] += 1
                    eng.wait_ge(S["c_sv"], sv[0])
                    eng.tensor_scalar(lncols[:, 10:11], lncols[:, 9:10], -1.0,
                                      None, ALU.mult).then_inc(S["c_gneg"], 1)
                    eng.wait_ge(S["c_gexp"], idx + 1)
                    eng.reciprocal(lncols[:, 12:13], lncols[:, 11:12]
                                   ).then_inc(S["c_sv"], 1)
                    sv[0] += 1
                    eng.wait_ge(S["c_sv"], sv[0])
                    gsc = sq_scr[:, 0:8]
                    scr2 = sq_scr[:, 8:16]
                    eng.tensor_scalar(gsc, gl4[:, 0:8], lncols[:, 12:13], None,
                                      ALU.mult).then_inc(S["c_sv"], 1)
                    sv[0] += 1
                    eng.wait_ge(S["c_sv"], sv[0])
                    eng.reduce_max(lncols[:, 9:10], gsc, axis=AX.X
                                   ).then_inc(S["c_sv"], 1)
                    sv[0] += 1
                    eng.wait_ge(S["c_sv"], sv[0])
                    eng.tensor_scalar(scr2, gsc, lncols[:, 9:10], None,
                                      ALU.is_equal).then_inc(S["c_sv"], 1)
                    sv[0] += 1
                    eng.wait_ge(S["c_sv"], sv[0])
                    eng.scalar_tensor_tensor(scr2, scr2, -2.0, gsc,
                                             ALU.mult, ALU.add
                                             ).then_inc(S["c_sv"], 1)
                    sv[0] += 1
                    eng.wait_ge(S["c_sv"], sv[0])
                    eng.reduce_max(lncols[:, 10:11], scr2, axis=AX.X
                                   ).then_inc(S["c_sv"], 1)
                    sv[0] += 1
                    eng.wait_ge(S["c_sv"], sv[0])
                    eng.tensor_scalar(scr2, gsc, lncols[:, 10:11], None,
                                      ALU.is_ge).then_inc(S["c_sv"], 1)
                    sv[0] += 1
                    eng.wait_ge(S["c_sv"], sv[0])
                    eng.tensor_mul(gsc, gsc, scr2).then_inc(S["c_sv"], 1)
                    sv[0] += 1
                    eng.wait_ge(S["c_sv"], sv[0])
                    eng.tensor_scalar(lncols[:, 13:14], lncols[:, 9:10],
                                      lncols[:, 10:11], 1e-8, ALU.add, ALU.add
                                      ).then_inc(S["c_sv"], 1)
                    sv[0] += 1
                    eng.wait_ge(S["c_sv"], sv[0])
                    eng.reciprocal(lncols[:, 14:15], lncols[:, 13:14]
                                   ).then_inc(S["c_sv"], 1)
                    sv[0] += 1
                    eng.wait_ge(S["c_sv"], sv[0])
                    gtb = 4 * u + tb
                    eng.tensor_scalar(w_tm[:, 8 * gtb:8 * (gtb + 1)], gsc,
                                      lncols[:, 14:15], None, ALU.mult
                                      ).then_inc(S["c_gw"], 1)

            if not FFN_ON:
                return
            eng.wait_ge(S["c_gw"], 4 * NU)
            for q in range(NQ):
                for e in range(E):
                    ge = 8 * q + e
                    for m in range(16):
                        gvi = 16 * ge + m
                        eng.wait_ge(S["c_sil"], gvi + 1)
                        eng.wait_ge(S["c_gv"], gvi + 1)
                        eng.tensor_mul(fprm[:, 1024 * m:1024 * m + 512],
                                       sil[:, 0:512], P[4][:, :])
                        eng.tensor_mul(fprm[:, 1024 * m + 512:1024 * (m + 1)],
                                       sil[:, 512:1024], P[5][:, :]
                                       ).then_inc(S["c_fp"], 1)
                    for tb in range(8):
                        di = 8 * ge + tb
                        pds = (P[0], P[1]) if tb % 2 == 0 else (P[2], P[3])
                        eng.wait_ge(S["c_dmm"], di + 1)
                        eng.wait_ge(S["s_ac"], 16 * (8 * q + tb + 1))
                        wcol = w_tm[:, 8 * (8 * q + tb) + e:
                                    8 * (8 * q + tb) + e + 1]
                        a0 = acc[:, 1024 * tb:1024 * tb + 512]
                        a1 = acc[:, 1024 * tb + 512:1024 * (tb + 1)]
                        eng.scalar_tensor_tensor(a0, pds[0][:, :], wcol, a0,
                                                 ALU.mult, ALU.add)
                        eng.scalar_tensor_tensor(a1, pds[1][:, :], wcol, a1,
                                                 ALU.mult, ALU.add
                                                 ).then_inc(S["c_dacc"], 1)

        @blk.scalar
        def _(eng):
            eng.wait_ge(S["c_pre"], 5)
            eng.activation(frows[0:1, 0:128], frows[0:1, 0:128], AF.Exp,
                           scale=-LN_TH / 32.0).then_inc(S["c_spre"], 1)
            for idx in range(8):
                eng.wait_ge(S["c_rope"], idx + 1)
                s_, w_ = idx // 2, idx % 2
                tab = Ctab if w_ == 0 else Stab
                eng.activation(tab[:, 512 * s_:512 * (s_ + 1)], rope_u[:],
                               AF.Sin, scale=TWO_PI).then_inc(S["c_spre"], 1)

            for u in range(NUE):
                j = u % NJ
                nsb = nsb_of(u)
                for tb in range(NTB):
                    eng.wait_ge(S["c_lnv"], 8 * u + tb + 1)
                    eng.activation(lncols[:, 6:7], lncols[:, 5:6], AF.Sqrt
                                   ).then_inc(S["c_lna"], 1)
                for gi in range(24):
                    proj, hp = gi // 8, gi % 8
                    gno = 24 * u + gi
                    eng.wait_ge(S["c_qkv"], gno + 1)
                    sl = slice(512 * (hp % 2), 512 * (hp % 2) + 512)
                    if proj == 0:
                        if hp >= 1:
                            eng.wait_ge(S["c_rope"], 8 + 16 * u + hp - 1)
                        eng.copy(qbf[:, sl], P[gi % 3][:, :]).then_inc(
                            S["c_evQKV"], 1)
                    elif proj == 1:
                        if hp >= 1:
                            eng.wait_ge(S["c_rope"], 8 + 16 * u + 8 + hp - 1)
                        eng.copy(kbf[:, sl], P[gi % 3][:, :]).then_inc(
                            S["c_evQKV"], 1)
                    else:
                        if hp >= 1:
                            eng.wait_ge(S["c_trV"], 8 * u + hp - 1)
                        eng.copy(vbf[:, sl], P[gi % 3][:, :]).then_inc(
                            S["c_evQKV"], 1)
                for h in range(16 if PH_B else 0):
                    eng.wait_ge(S["c_sc"], SC_CUM(u, h) + 1)
                    eng.wait_ge(S["c_av"], AV_CUM(u, h))
                    eng.wait_ge(S["c_nrm"], NRM_CUM(u, h))
                    eng.activation(ptl[:, :], P[6][0:8, :], AF.Exp,
                                   scale=0.125).then_inc(S["c_exp"], 1)
                    for sb_ in range(nsb):
                        eng.wait_ge(S["c_sc"], SC_CUM(u, h) + sb_ + 2)
                        if sb_ >= 4:
                            eng.wait_ge(S["c_av"], AV_CUM(u, h) + sb_ - 3)
                        pt = (pt0, pt1, pt2, pt3)[sb_ % 4]
                        eng.activation(pt[:, :], P[(4, 5, 0)[sb_ % 3]][:, :],
                                       AF.Exp, scale=0.125
                                       ).then_inc(S["c_exp"], 1)
                for c in range(8 if PH_C else 0):
                    oidx = 8 * u + c
                    eng.wait_ge(S["c_opmm"], oidx + 1)
                    if oidx > 0:
                        eng.wait_ge(S["c_trC"], oidx)
                    eng.copy(aof[:], P[oidx % 2][:, :]).then_inc(S["c_opev"], 1)
                for tb in range(NTB if (PH_C and PH_G) else 0):
                    idx = 4 * u + tb
                    eng.wait_ge(S["c_lnv"], 8 * u + 4 + tb + 1)
                    eng.activation(lncols[:, 6:7], lncols[:, 5:6], AF.Sqrt
                                   ).then_inc(S["c_lna"], 1)
                    for half in range(2):
                        bidx = 2 * idx + half
                        eng.wait_ge(S["c_trC2"], bidx + 1)
                        eng.wait_ge(S["c_evC2d"], bidx + 1)
                        if PH_GATE and idx > 0 and half == 0:
                            eng.wait_ge(S["c_gw"], idx)
                        if PH_GATE and bidx > 0:
                            eng.wait_ge(S["c_gmm"], bidx)
                        last = None
                        if PH_EV2:
                            for cc in range(4):
                                last = eng.copy(
                                    gl4[:, 128 * cc:128 * (cc + 1)],
                                    P[6][:, 128 * cc:128 * (cc + 1)])
                            last.then_inc(S["c_evC2a"], 1)
                        else:
                            eng.nop().then_inc(S["c_evC2a"], 1)
                    if PH_GATE:
                        eng.wait_ge(S["c_gneg"], idx + 1)
                        eng.wait_ge(S["c_gmm"], 2 * (idx + 1))
                        eng.wait_ge(S["c_evC2a"], 2 * (idx + 1))
                        eng.activation(gl4[:, 0:8], P[2][:, 0:8], AF.Exp,
                                       bias=lncols[:, 10:11],
                                       accum_out=lncols[:, 11:12]
                                       ).then_inc(S["c_gexp"], 1)

            if not FFN_ON:
                return
            for q in range(NQ):
                for e in range(E):
                    ge = 8 * q + e
                    for m in range(16):
                        gvi = 16 * ge + m
                        pgs = (P[0], P[1]) if m % 2 == 0 else (P[2], P[3])
                        eng.wait_ge(S["c_gg"], gvi + 1)
                        if gvi > 0:
                            eng.wait_ge(S["c_fp"], gvi)
                        eng.activation(sil[:, 0:512], pgs[0][:, :], AF.Silu)
                        eng.activation(sil[:, 512:1024], pgs[1][:, :], AF.Silu
                                       ).then_inc(S["c_sil"], 1)

    return nc


_NC_CACHE = {}


def kernel(L0_attn_x, L0_attn_Wq, L0_attn_Wk, L0_attn_Wv, L0_attn_lat_k,
           L0_attn_lat_v, L0_attn_W_lk, L0_attn_W_lv, L0_attn_W_out,
           L0_ffn_Wg, L0_ffn_W11, L0_ffn_W12, L0_ffn_W2):
    x = np.ascontiguousarray(np.asarray(L0_attn_x, dtype=np.float32))
    weights = {
        "Wq": np.ascontiguousarray(L0_attn_Wq, dtype=np.float32),
        "Wk": np.ascontiguousarray(L0_attn_Wk, dtype=np.float32),
        "Wv": np.ascontiguousarray(L0_attn_Wv, dtype=np.float32),
        "Wout": np.ascontiguousarray(L0_attn_W_out, dtype=np.float32),
        "lat_k": np.ascontiguousarray(L0_attn_lat_k, dtype=np.float32),
        "lat_v": np.ascontiguousarray(L0_attn_lat_v, dtype=np.float32),
        "W_lk": np.ascontiguousarray(L0_attn_W_lk, dtype=np.float32),
        "W_lv": np.ascontiguousarray(L0_attn_W_lv, dtype=np.float32),
        "Wg": np.ascontiguousarray(L0_ffn_Wg, dtype=np.float32),
        "W11": np.ascontiguousarray(L0_ffn_W11, dtype=np.float32),
        "W12": np.ascontiguousarray(L0_ffn_W12, dtype=np.float32),
        "W2": np.ascontiguousarray(L0_ffn_W2, dtype=np.float32),
    }
    if "nc" not in _NC_CACHE:
        _NC_CACHE["nc"] = build_program()
    nc = _NC_CACHE["nc"]
    in_maps = []
    for core in range(8):
        m = {"x": np.ascontiguousarray(x[2 * core:2 * core + 2])}
        m.update(weights)
        in_maps.append(m)
    res = run_bass_kernel_spmd(nc, in_maps, core_ids=list(range(8)))
    out = np.concatenate([r["y"] for r in res.results], axis=0)
    return out.astype(np.float32)
